# version 02
# baseline (speedup 1.0000x reference)
import contextlib
import numpy as np
import ml_dtypes
import concourse.bass as bass
import concourse.mybir as mybir
from concourse.bass_utils import run_bass_kernel_spmd

F32 = mybir.dt.float32
BF16 = mybir.dt.bfloat16
I32 = mybir.dt.int32
AF = mybir.ActivationFunctionType
ALU = mybir.AluOpType
AX = mybir.AxisListType
BF = ml_dtypes.bfloat16

T = 2048
D = 1024
DI = 2048
NS = 16
NCORE = 8
CAP = 2560
E0 = 8
E1 = 6
DE1 = 512
NH = 8
HD = 128
TWOPI = 6.283185307179586
ENGS = ("sync", "tensor", "vector", "scalar", "gpsimd")


class B:

    def __init__(self, nc):
        self.nc = nc
        self.ops = {e: [] for e in ENGS}
        self.cnt = {e: 0 for e in ENGS}
        self.wm = {e: {o: 0 for o in ENGS} for e in ENGS}
        self.regions = {}
        self.pending_reads = {e: [] for e in ENGS}

    def _need(self, eng, waits, dep):
        deng, dcnt = dep
        if dcnt <= self.wm[eng][deng]:
            return
        waits.append((deng, dcnt))
        self.wm[eng][deng] = dcnt

    def op(self, eng, emit, reads=(), writes=(), inc=1):
        waits = []
        for r in reads:
            st = self.regions.get(r)
            if st and st["w"]:
                self._need(eng, waits, st["w"])
        for w in writes:
            st = self.regions.get(w)
            if st:
                if st["w"]:
                    self._need(eng, waits, st["w"])
                for reng, rcnt in st["r"].items():
                    self._need(eng, waits, (reng, rcnt))
        self.cnt[eng] += inc
        after = self.cnt[eng]
        self.ops[eng].append((waits, emit, inc))
        if inc:
            for r in list(self.pending_reads[eng]) + list(reads):
                st = self.regions.setdefault(r, {"w": None, "r": {}})
                st["r"][eng] = max(st["r"].get(eng, 0), after)
            self.pending_reads[eng] = []
            for w in writes:
                self.regions[w] = {"w": (eng, after), "r": {}}
        else:
            self.pending_reads[eng].extend(reads)
            assert not writes, "quiet ops must not declare writes"
        return after

    def dma(self, eng, out, in_, reads=(), writes=(), **kw):
        return self.op(eng, lambda E: E.dma_start(out=out, in_=in_, **kw),
                       reads=reads, writes=writes, inc=16)

    def finish(self):
        nc = self.nc
        with contextlib.ExitStack() as st:
            sems = {e: st.enter_context(nc.semaphore(f"s_{e}")) for e in ENGS}
            st.enter_context(nc.Block()).__class__
        raise RuntimeError("use finish_into instead")

    def finish_into(self, stack):
        nc = self.nc
        sems = {e: stack.enter_context(nc.semaphore(f"s_{e}")) for e in ENGS}
        blk = stack.enter_context(nc.Block())
        final = {e: self.cnt[e] for e in ENGS}
        for eng in ENGS:
            ops = self.ops[eng]

            def section(E, eng=eng, ops=ops):
                for waits, emit, inc in ops:
                    for deng, dcnt in waits:
                        E.wait_ge(sems[deng], dcnt)
                    ins = emit(E)
                    if inc:
                        ins.then_inc(sems[eng], inc)
                if eng == "sync":
                    for o in ENGS:
                        if final[o]:
                            E.wait_ge(sems[o], final[o])

            getattr(blk, eng)(section)



def build_A(upto='full'):
    nc = bass.Bass("TRN2", target_bir_lowering=False, debug=False)
    x_in = nc.dram_tensor("x", [T, D], F32, kind="ExternalInput").ap()
    wr_in = nc.dram_tensor("w_router", [128, E0 * 8], F32, kind="ExternalInput").ap()
    br_in = nc.dram_tensor("b_router", [E0], F32, kind="ExternalInput").ap()
    eye_in = nc.dram_tensor("eye", [128, 128], F32, kind="ExternalInput").ap()
    triu_in = nc.dram_tensor("triu", [128, 128], F32, kind="ExternalInput").ap()
    ones_in = nc.dram_tensor("ones", [128, 128], F32, kind="ExternalInput").ap()

    xnT_f_out = nc.dram_tensor("xnT_f", [D, T], F32, kind="ExternalOutput")
    xnT_b_out = nc.dram_tensor("xnT_b", [D, T], BF16, kind="ExternalOutput")
    route_out = nc.dram_tensor("route", [T, 20], F32, kind="ExternalOutput").ap()
    counts_out = nc.dram_tensor("counts", [1, E0], F32, kind="ExternalOutput").ap()
    xnTf_hbm = xnT_f_out.ap().rearrange("(g p) t -> p g t", p=128)
    xnTb_hbm = xnT_b_out.ap().rearrange("(g p) t -> p g t", p=128)

    b = B(nc)
    import os
    NT = int(os.environ.get('NTILES', T // 128))
    with contextlib.ExitStack() as st:
        e = st.enter_context
        xt = [e(nc.sbuf_tensor(f"xt{i}", [128, D], F32)) for i in range(2)]
        eye = e(nc.sbuf_tensor("sb_eye", [128, 128], F32))
        triu = e(nc.sbuf_tensor("sb_triu", [128, 128], F32))
        ones = e(nc.sbuf_tensor("sb_ones", [128, 128], F32))
        wr = e(nc.sbuf_tensor("sb_wr", [128, E0 * 8], F32))
        br_rep = e(nc.sbuf_tensor("br_rep", [128, E0], F32))
        stat = e(nc.sbuf_tensor("stat", [128, 8], F32))
        xsq = e(nc.sbuf_tensor("xsq", [128, D], F32))
        xnf = e(nc.sbuf_tensor("xnf", [128, D], F32))
        xnTf = e(nc.sbuf_tensor("xnTf", [128, D], F32))
        xnTb = e(nc.sbuf_tensor("xnTb", [128, D], BF16))
        gat = e(nc.sbuf_tensor("gat", [128, E0], F32))
        oh1 = e(nc.sbuf_tensor("oh1", [128, E0], F32))
        oh2 = e(nc.sbuf_tensor("oh2", [128, E0], F32))
        oht = e(nc.sbuf_tensor("oht", [128, E0], F32))
        posx = e(nc.sbuf_tensor("posx", [128, E0], F32))
        scr = e(nc.sbuf_tensor("scr", [128, E0], F32))
        scr2 = e(nc.sbuf_tensor("scr2", [128, E0], F32))
        rpack = e(nc.sbuf_tensor("rpack", [128, 20], F32))
        carry = e(nc.sbuf_tensor("carry", [1, E0], F32))
        onecol = e(nc.sbuf_tensor("onecol", [128, E0], F32))
        cvals = e(nc.sbuf_tensor("cvals", [128, 2], F32))
        ps_t = e(nc.psum_tensor("ps_t", [128, 512], F32))
        ps_l = e(nc.psum_tensor("ps_l", [128, E0], F32))
        ps_p = e(nc.psum_tensor("ps_p", [128, E0], F32))
        ps_c = e(nc.psum_tensor("ps_c", [1, E0], F32))
        ps_b = e(nc.psum_tensor("ps_b", [128, E0], F32))

        b.dma("sync", eye[:], eye_in, writes=("eye",))
        b.dma("sync", triu[:], triu_in, writes=("triu",))
        b.dma("sync", ones[:], ones_in, writes=("ones",))
        b.dma("sync", wr[:], wr_in, writes=("wr",))
        b.dma("sync", br_rep[0:1, :], br_in.rearrange("(o e) -> o e", o=1), writes=("br",))
        b.op("tensor", lambda E: E.matmul(ps_b[:], ones[0:1, :], br_rep[0:1, :],
                                          start=True, stop=True),
             reads=("ones", "br"), writes=("ps_b",))
        b.op("vector", lambda E: E.tensor_copy(br_rep[:], ps_b[:]),
             reads=("ps_b",), writes=("br_rep",))
        b.op("vector", lambda E: E.memset(carry[:], 0.0), writes=("carry",))
        b.op("vector", lambda E: E.memset(onecol[:], 1.0), writes=("onecol",))
        b.op("vector", lambda E: E.memset(cvals[:, 0:1], 1e-5), writes=("cvals",))

        for i in range(NT):
            cur = xt[i % 2]
            xk = f"xt{i % 2}"
            b.dma("sync", cur[:], x_in[i * 128:(i + 1) * 128, :], writes=(xk,))
            b.op("vector", lambda E, cur=cur: E.tensor_reduce(
                stat[:, 0:1], cur[:], axis=AX.X, op=ALU.add),
                reads=(xk,), writes=("stat",))
            b.op("scalar", lambda E, cur=cur: E.activation(
                xsq[:], cur[:], AF.Square), reads=(xk,), writes=("xsq",))
            b.op("vector", lambda E: E.tensor_reduce(
                stat[:, 1:2], xsq[:], axis=AX.X, op=ALU.add),
                reads=("xsq",), writes=("stat",))
            b.op("vector", lambda E: E.tensor_scalar(
                stat[:, 2:3], stat[:, 0:1], 1.0 / D, None, ALU.mult),
                reads=("stat",), writes=("stat",))
            b.op("vector", lambda E: E.tensor_scalar(
                stat[:, 3:4], stat[:, 1:2], 1.0 / D, None, ALU.mult),
                reads=("stat",), writes=("stat",))
            b.op("vector", lambda E: E.tensor_mul(
                stat[:, 4:5], stat[:, 2:3], stat[:, 2:3]),
                reads=("stat",), writes=("stat",))
            b.op("vector", lambda E: E.tensor_sub(
                stat[:, 4:5], stat[:, 3:4], stat[:, 4:5]),
                reads=("stat",), writes=("stat",))
            b.op("scalar", lambda E: E.activation(
                stat[:, 5:6], stat[:, 4:5], AF.Ln, bias=cvals[:, 0:1]),
                reads=("stat", "cvals"), writes=("stat",))
            b.op("scalar", lambda E: E.activation(
                stat[:, 5:6], stat[:, 5:6], AF.Exp, scale=-0.5),
                reads=("stat",), writes=("stat",))
            b.op("vector", lambda E: E.tensor_mul(
                stat[:, 6:7], stat[:, 2:3], stat[:, 5:6]),
                reads=("stat",), writes=("stat",))
            b.op("vector", lambda E: E.tensor_scalar(
                stat[:, 6:7], stat[:, 6:7], -1.0, None, ALU.mult),
                reads=("stat",), writes=("stat",))
            b.op("vector", lambda E, cur=cur: E.tensor_scalar(
                xnf[:], cur[:], stat[:, 5:6], stat[:, 6:7], ALU.mult, ALU.add),
                reads=(xk, "stat"), writes=("xnf",))
            if upto == 'stat':
                b.dma("sync", xnT_f_out.ap()[0:128, 0:D], xnf[:], reads=("xnf",))
                continue
            for jj in (0, 4):
                for j in range(jj, jj + 4):
                    b.op("tensor", lambda E, j=j, jj=jj: E.transpose(
                        ps_t[:, (j - jj) * 128:(j - jj) * 128 + 128],
                        xnf[:, j * 128:(j + 1) * 128], eye[:]),
                        reads=("xnf", "eye"), writes=("ps_t",))
                for j in range(jj, jj + 4):
                    b.op("scalar", lambda E, j=j, jj=jj: E.activation(
                        xnTf[:, j * 128:(j + 1) * 128],
                        ps_t[:, (j - jj) * 128:(j - jj) * 128 + 128], AF.Copy),
                        reads=("ps_t",), writes=("xnTf",))
            b.op("vector", lambda E: E.tensor_copy(xnTb[:], xnTf[:]),
                 reads=("xnTf",), writes=("xnTb",))
            b.dma("sync", xnTf_hbm[:, :, i * 128:(i + 1) * 128],
                  xnTf[:].rearrange("p (g t) -> p g t", g=8), reads=("xnTf",))
            if upto in ('notb', 'trx'):
                continue
            b.dma("sync", xnTb_hbm[:, :, i * 128:(i + 1) * 128],
                  xnTb[:].rearrange("p (g t) -> p g t", g=8), reads=("xnTb",))
            if upto == 'ln':
                continue
            for k in range(8):
                b.op("tensor", lambda E, k=k: E.matmul(
                    ps_l[:], xnTf[:, k * 128:(k + 1) * 128],
                    wr[:, k * 8:(k + 1) * 8],
                    start=(k == 0), stop=(k == 7)),
                    reads=("xnTf", "wr"),
                    writes=("ps_l",) if k in (0, 7) else (),
                    inc=1 if k in (0, 7) else 0)
            b.op("vector", lambda E: E.tensor_add(gat[:], ps_l[:], br_rep[:]),
                 reads=("ps_l", "br_rep"), writes=("gat",))
            b.op("vector", lambda E: E.reduce_max(stat[:, 7:8], gat[:], axis=AX.X),
                 reads=("gat",), writes=("stat",))
            b.op("vector", lambda E: E.tensor_scalar(
                stat[:, 7:8], stat[:, 7:8], -1.0, None, ALU.mult),
                reads=("stat",), writes=("stat",))
            b.op("scalar", lambda E: E.activation(
                gat[:], gat[:], AF.Exp, bias=stat[:, 7:8]),
                reads=("gat", "stat"), writes=("gat",))
            b.op("vector", lambda E: E.tensor_reduce(
                stat[:, 7:8], gat[:], axis=AX.X, op=ALU.add),
                reads=("gat",), writes=("stat",))
            b.op("vector", lambda E: E.reciprocal(stat[:, 7:8], stat[:, 7:8]),
                 reads=("stat",), writes=("stat",))
            b.op("vector", lambda E: E.tensor_scalar(
                gat[:], gat[:], stat[:, 7:8], None, ALU.mult),
                reads=("gat", "stat"), writes=("gat",))

            def top1(dst, dstk, src, srck, gcol):
                b.op("vector", lambda E: E.reduce_max(
                    stat[:, gcol:gcol + 1], src[:], axis=AX.X),
                    reads=(srck,), writes=("stat",))
                b.op("vector", lambda E: E.tensor_scalar(
                    dst[:], src[:], stat[:, gcol:gcol + 1], None, ALU.is_equal),
                    reads=(srck, "stat"), writes=(dstk,))
                b.op("vector", lambda E: E.tensor_tensor_scan(
                    scr[:], onecol[:], dst[:], 0.0, ALU.mult, ALU.add),
                    reads=(dstk, "onecol"), writes=("scr",))
                b.op("vector", lambda E: E.tensor_scalar(
                    scr[:], scr[:], 1.0, None, ALU.is_equal),
                    reads=("scr",), writes=("scr",))
                b.op("vector", lambda E: E.tensor_mul(dst[:], dst[:], scr[:]),
                     reads=(dstk, "scr"), writes=(dstk,))

            if upto == 'router':
                continue
            top1(oh1, "oh1", gat, "gat", 0)
            b.op("vector", lambda E: E.scalar_tensor_tensor(
                scr2[:], oh1[:], -1e30, gat[:], ALU.mult, ALU.add),
                reads=("oh1", "gat"), writes=("scr2",))
            top1(oh2, "oh2", scr2, "scr2", 1)
            b.op("vector", lambda E: E.tensor_add(oht[:], oh1[:], oh2[:]),
                 reads=("oh1", "oh2"), writes=("oht",))
            if upto == 'top2':
                continue
            b.op("tensor", lambda E: E.matmul(ps_p[:], triu[:], oht[:],
                                              start=True, stop=False),
                 reads=("triu", "oht"), writes=("ps_p",))
            b.op("tensor", lambda E: E.matmul(ps_p[:], ones[0:1, :], carry[:],
                                              start=False, stop=True),
                 reads=("ones", "carry"), writes=("ps_p",))
            b.op("vector", lambda E: E.tensor_copy(posx[:], ps_p[:]),
                 reads=("ps_p",), writes=("posx",))
            b.op("tensor", lambda E: E.matmul(ps_c[:], ones[:, 0:1], oht[:],
                                              start=True, stop=True),
                 reads=("ones", "oht"), writes=("ps_c",))
            b.op("vector", lambda E: E.tensor_add(carry[:], carry[:], ps_c[:]),
                 reads=("ps_c", "carry"), writes=("carry",))
            b.op("vector", lambda E: E.tensor_mul(scr[:], posx[:], oh1[:]),
                 reads=("posx", "oh1"), writes=("scr",))
            b.op("vector", lambda E: E.tensor_reduce(
                stat[:, 2:3], scr[:], axis=AX.X, op=ALU.add),
                reads=("scr",), writes=("stat",))
            b.op("vector", lambda E: E.tensor_add(scr[:], posx[:], oh1[:]),
                 reads=("posx", "oh1"), writes=("scr",))
            b.op("vector", lambda E: E.tensor_mul(scr[:], scr[:], oh2[:]),
                 reads=("scr", "oh2"), writes=("scr",))
            b.op("vector", lambda E: E.tensor_reduce(
                stat[:, 3:4], scr[:], axis=AX.X, op=ALU.add),
                reads=("scr",), writes=("stat",))
            b.op("vector", lambda E: E.tensor_copy(rpack[:, 0:8], oh1[:]),
                 reads=("oh1",), writes=("rpack",))
            b.op("vector", lambda E: E.tensor_copy(rpack[:, 8:16], oh2[:]),
                 reads=("oh2",), writes=("rpack",))
            b.op("vector", lambda E: E.tensor_copy(rpack[:, 16:20], stat[:, 0:4]),
                 reads=("stat",), writes=("rpack",))
            b.dma("sync", route_out[i * 128:(i + 1) * 128, :], rpack[:],
                  reads=("rpack",))
        b.dma("sync", counts_out, carry[:], reads=("carry",))
        b.finish_into(st)
    return nc




def build_B():
    nc = bass.Bass("TRN2", target_bir_lowering=False, debug=False)
    CH = 512
    NCH = T // CH
    ap = lambda name, shape, dt_, kind: nc.dram_tensor(name, shape, dt_, kind=kind).ap()
    xnb_in = nc.dram_tensor("xnT_b", [D, T], BF16, kind="ExternalInput")
    xnb_hbm = xnb_in.ap().rearrange("(g p) t -> p g t", p=128)
    wip_in = ap("wipT", [D, 4096], F32, "ExternalInput")
    wxp_in = ap("wxpP", [128, 16 * 96], F32, "ExternalInput")
    wdt_in = ap("wdtT", [64, DI], F32, "ExternalInput")
    wop_in = ap("wopT", [DI, D], F32, "ExternalInput")
    wglu_in = ap("wgluT", [D, DI], F32, "ExternalInput")
    conv_in = ap("convP", [128, 64], F32, "ExternalInput")
    convb_in = ap("convbP", [128, 16], F32, "ExternalInput")
    dtb_in = ap("dtbP", [128, 16], F32, "ExternalInput")
    alog_in = ap("alogP", [128, 256], F32, "ExternalInput")
    dv_in = ap("DP", [128, 16], F32, "ExternalInput")
    glub_in = ap("glubP", [128, 16], F32, "ExternalInput")
    lng_in = ap("lngP", [128, 8], F32, "ExternalInput")
    lnb_in = ap("lnbP", [128, 8], F32, "ExternalInput")
    sel_in = ap("sel16", [16, 16 * 128], F32, "ExternalInput")
    sT_out = nc.dram_tensor("sT", [D, T], BF16, kind="ExternalOutput")
    sT_hbm = sT_out.ap().rearrange("(g p) t -> p g t", p=128)

    b = B(nc)
    with contextlib.ExitStack() as st:
        e = st.enter_context
        xn2 = e(nc.sbuf_tensor("xn2", [128, 8 * CH], BF16))
        wp = [e(nc.sbuf_tensor(f"wp{i}", [128, 2048], BF16)) for i in range(8)]
        wxp = e(nc.sbuf_tensor("wxp", [128, 16 * 96], BF16))
        wdt = e(nc.sbuf_tensor("wdt", [64, DI], BF16))
        cw = e(nc.sbuf_tensor("cw", [128, 64], F32))
        cb = e(nc.sbuf_tensor("cb", [128, 16], F32))
        dtbc = e(nc.sbuf_tensor("dtbc", [128, 16], F32))
        Aneg = e(nc.sbuf_tensor("Aneg", [128, 256], F32))
        Dv = e(nc.sbuf_tensor("Dv", [128, 16], F32))
        glub = e(nc.sbuf_tensor("glub", [128, 16], F32))
        lng = e(nc.sbuf_tensor("lng", [128, 8], F32))
        lnb = e(nc.sbuf_tensor("lnb", [128, 8], F32))
        sel = e(nc.sbuf_tensor("sel", [16, 16 * 128], F32))
        tails = e(nc.sbuf_tensor("tails", [128, 48], BF16))
        xpct = e(nc.sbuf_tensor("xpct", [128, 515], BF16))
        convacc = e(nc.sbuf_tensor("convacc", [128, CH], F32))
        xc = e(nc.sbuf_tensor("xc", [128, 16 * CH], BF16))
        siluz = e(nc.sbuf_tensor("siluz", [128, 16 * CH], BF16))
        dtraw = e(nc.sbuf_tensor("dtraw", [64, CH], BF16))
        Bmt = e(nc.sbuf_tensor("Bmt", [16, CH], F32))
        Cmt = e(nc.sbuf_tensor("Cmt", [16, CH], F32))
        brep = e(nc.sbuf_tensor("brep", [128, 16 * CH], BF16))
        crep = e(nc.sbuf_tensor("crep", [128, 16 * CH], BF16))
        qsp = e(nc.sbuf_tensor("qsp", [128, CH], F32))
        spa = e(nc.sbuf_tensor("spa", [128, CH], F32))
        spb = e(nc.sbuf_tensor("spb", [128, CH], F32))
        dtbg = e(nc.sbuf_tensor("dtbg", [128, CH], BF16))
        dtu = e(nc.sbuf_tensor("dtu", [128, CH], BF16))
        dA_t = e(nc.sbuf_tensor("dA_t", [128, CH], BF16))
        dBu_t = e(nc.sbuf_tensor("dBu_t", [128, CH], BF16))
        h_t = e(nc.sbuf_tensor("h_t", [128, CH], BF16))
        hc_t = e(nc.sbuf_tensor("hc_t", [128, CH], BF16))
        yacc = e(nc.sbuf_tensor("yacc", [128, CH], F32))
        hcarry = e(nc.sbuf_tensor("hcarry", [128, 256], F32))
        y3 = e(nc.sbuf_tensor("y3", [128, 16 * CH], BF16))
        mT = e(nc.sbuf_tensor("mT", [128, 8 * CH], BF16))
        g1 = e(nc.sbuf_tensor("g1", [128, 8 * CH], BF16))
        sg2 = e(nc.sbuf_tensor("sg2", [128, 8 * CH], BF16))
        s_t = e(nc.sbuf_tensor("s_t", [128, 8 * CH], BF16))
        onescol = e(nc.sbuf_tensor("onescol", [128, 1], F32))
        psA = e(nc.psum_tensor("psA", [128, 512], F32))
        psB = e(nc.psum_tensor("psB", [64, 512], F32))
        psC = e(nc.psum_tensor("psC", [16, 512], F32))
        psD = e(nc.psum_tensor("psD", [16, 512], F32))
        psE = e(nc.psum_tensor("psE", [128, 512], F32))
        psF = e(nc.psum_tensor("psF", [128, 512], F32))
        psG = e(nc.psum_tensor("psG", [128, 512], F32))

        b.dma("gpsimd", wxp[:], wxp_in, writes=("wxp",))
        b.dma("gpsimd", wdt[:], wdt_in, writes=("wdt",))
        for t_, a_, k_ in [(cw, conv_in, "cw"), (cb, convb_in, "cb"),
                           (dtbc, dtb_in, "dtbc"), (Aneg, alog_in, "Aneg"),
                           (Dv, dv_in, "Dv"), (glub, glub_in, "glub"),
                           (lng, lng_in, "lng"), (lnb, lnb_in, "lnb"),
                           (sel, sel_in, "sel")]:
            b.dma("sync", t_[:], a_, writes=(k_,))
        b.op("scalar", lambda E: E.activation(Aneg[:], Aneg[:], AF.Exp),
             reads=("Aneg",), writes=("Aneg",))
        b.op("vector", lambda E: E.tensor_scalar(
            Aneg[:], Aneg[:], -1.0, None, ALU.mult),
            reads=("Aneg",), writes=("Aneg",))
        b.op("vector", lambda E: E.memset(tails[:], 0.0), writes=("tails",))
        b.op("vector", lambda E: E.memset(hcarry[:], 0.0), writes=("hcarry",))
        b.op("vector", lambda E: E.memset(onescol[:], 1.0), writes=("onescol",))

        for ci in range(NCH):
            c0 = ci * CH
            b.dma("sync", xn2[:].rearrange("p (g t) -> p g t", g=8),
                  xnb_hbm[:, :, c0:c0 + CH], writes=("xn2",))
            for g in range(8):
                b.op("vector", lambda E, g=g: E.tensor_scalar(
                    xn2[:, g * CH:(g + 1) * CH], xn2[:, g * CH:(g + 1) * CH],
                    lng[:, g:g + 1], lnb[:, g:g + 1], ALU.mult, ALU.add),
                    reads=("xn2", "lng", "lnb"), writes=("xn2",))
            for half in range(2):
                for k in range(8):
                    b.dma("gpsimd", wp[k][:],
                          wip_in[k * 128:(k + 1) * 128,
                                 half * 2048:half * 2048 + 2048],
                          writes=(f"wp{k}",))
                for ogl in range(16):
                    og = half * 16 + ogl
                    for k in range(8):
                        b.op("tensor", lambda E, k=k, ogl=ogl: E.matmul(
                            psA[:], wp[k][:, ogl * 128:(ogl + 1) * 128],
                            xn2[:, k * CH:(k + 1) * CH],
                            start=(k == 0), stop=(k == 7)),
                            reads=(f"wp{k}", "xn2"),
                            writes=("psA",) if k in (0, 7) else (),
                            inc=1 if k in (0, 7) else 0)
                    if og < 16:
                        dg = og
                        b.op("scalar", lambda E: E.activation(
                            xpct[:, 3:515], psA[:], AF.Copy),
                            reads=("psA",), writes=("xpct",))
                        b.op("vector", lambda E, dg=dg: E.tensor_copy(
                            xpct[:, 0:3], tails[:, dg * 3:dg * 3 + 3]),
                            reads=("tails",), writes=("xpct",))
                        b.op("vector", lambda E, dg=dg: E.tensor_copy(
                            tails[:, dg * 3:dg * 3 + 3], xpct[:, 512:515]),
                            reads=("xpct",), writes=("tails",))
                        b.op("vector", lambda E, dg=dg: E.tensor_scalar(
                            convacc[:], xpct[:, 0:CH], cw[:, dg * 4:dg * 4 + 1],
                            None, ALU.mult),
                            reads=("xpct", "cw"), writes=("convacc",))
                        for kk in range(1, 4):
                            b.op("vector", lambda E, dg=dg, kk=kk: E.scalar_tensor_tensor(
                                convacc[:], xpct[:, kk:kk + CH],
                                cw[:, dg * 4 + kk:dg * 4 + kk + 1], convacc[:],
                                ALU.mult, ALU.add),
                                reads=("xpct", "cw", "convacc"), writes=("convacc",))
                        b.op("scalar", lambda E, dg=dg: E.activation(
                            xc[:, dg * CH:(dg + 1) * CH], convacc[:], AF.Silu,
                            bias=cb[:, dg:dg + 1]),
                            reads=("convacc", "cb"), writes=("xc",))
                    else:
                        zg = og - 16
                        b.op("scalar", lambda E, zg=zg: E.activation(
                            siluz[:, zg * CH:(zg + 1) * CH], psA[:], AF.Silu),
                            reads=("psA",), writes=("siluz",))
            for k in range(16):
                b.op("tensor", lambda E, k=k: E.matmul(
                    psB[:], wxp[:, k * 96:k * 96 + 64],
                    xc[:, k * CH:(k + 1) * CH],
                    start=(k == 0), stop=(k == 15)),
                    reads=("wxp", "xc"),
                    writes=("psB",) if k in (0, 15) else (),
                    inc=1 if k in (0, 15) else 0)
            for k in range(16):
                b.op("tensor", lambda E, k=k: E.matmul(
                    psC[:], wxp[:, k * 96 + 64:k * 96 + 80],
                    xc[:, k * CH:(k + 1) * CH],
                    start=(k == 0), stop=(k == 15)),
                    reads=("wxp", "xc"),
                    writes=("psC",) if k in (0, 15) else (),
                    inc=1 if k in (0, 15) else 0)
            for k in range(16):
                b.op("tensor", lambda E, k=k: E.matmul(
                    psD[:], wxp[:, k * 96 + 80:k * 96 + 96],
                    xc[:, k * CH:(k + 1) * CH],
                    start=(k == 0), stop=(k == 15)),
                    reads=("wxp", "xc"),
                    writes=("psD",) if k in (0, 15) else (),
                    inc=1 if k in (0, 15) else 0)
            b.op("scalar", lambda E: E.activation(dtraw[:], psB[:], AF.Copy),
                 reads=("psB",), writes=("dtraw",))
            b.op("scalar", lambda E: E.activation(Bmt[:], psC[:], AF.Copy),
                 reads=("psC",), writes=("Bmt",))
            b.op("scalar", lambda E: E.activation(Cmt[:], psD[:], AF.Copy),
                 reads=("psD",), writes=("Cmt",))
            for n in range(NS):
                b.op("tensor", lambda E, n=n: E.matmul(
                    psE[:], sel[:, n * 128:(n + 1) * 128], Bmt[:],
                    start=True, stop=True),
                    reads=("sel", "Bmt"), writes=("psE",))
                b.op("scalar", lambda E, n=n: E.activation(
                    brep[:, n * CH:(n + 1) * CH], psE[:], AF.Copy),
                    reads=("psE",), writes=("brep",))
                b.op("tensor", lambda E, n=n: E.matmul(
                    psF[:], sel[:, n * 128:(n + 1) * 128], Cmt[:],
                    start=True, stop=True),
                    reads=("sel", "Cmt"), writes=("psF",))
                b.op("scalar", lambda E, n=n: E.activation(
                    crep[:, n * CH:(n + 1) * CH], psF[:], AF.Copy),
                    reads=("psF",), writes=("crep",))
            for dg in range(16):
                b.op("tensor", lambda E, dg=dg: E.matmul(
                    psG[:], wdt[:, dg * 128:(dg + 1) * 128], dtraw[:],
                    start=True, stop=True),
                    reads=("wdt", "dtraw"), writes=("psG",))
                b.op("scalar", lambda E: E.activation(qsp[:], psG[:], AF.Copy),
                     reads=("psG",), writes=("qsp",))
                b.op("scalar", lambda E, dg=dg: E.activation(
                    spa[:], qsp[:], AF.Abs, bias=dtbc[:, dg:dg + 1]),
                    reads=("qsp", "dtbc"), writes=("spa",))
                b.op("scalar", lambda E: E.activation(
                    spa[:], spa[:], AF.Exp, scale=-1.0),
                    reads=("spa",), writes=("spa",))
                b.op("scalar", lambda E: E.activation(
                    spa[:], spa[:], AF.Ln, bias=onescol[:, 0:1]),
                    reads=("spa", "onescol"), writes=("spa",))
                b.op("vector", lambda E, dg=dg: E.tensor_scalar(
                    spb[:], qsp[:], dtbc[:, dg:dg + 1], 0.0, ALU.add, ALU.max),
                    reads=("qsp", "dtbc"), writes=("spb",))
                b.op("vector", lambda E: E.tensor_add(dtbg[:], spa[:], spb[:]),
                     reads=("spa", "spb"), writes=("dtbg",))
                b.op("vector", lambda E, dg=dg: E.tensor_mul(
                    dtu[:], dtbg[:], xc[:, dg * CH:(dg + 1) * CH]),
                    reads=("dtbg", "xc"), writes=("dtu",))
                b.op("vector", lambda E: E.memset(yacc[:], 0.0), writes=("yacc",))
                for n in range(NS):
                    b.op("scalar", lambda E, dg=dg, n=n: E.activation(
                        dA_t[:], dtbg[:], AF.Exp,
                        scale=Aneg[:, dg * 16 + n:dg * 16 + n + 1]),
                        reads=("dtbg", "Aneg"), writes=("dA_t",))
                    b.op("vector", lambda E, n=n: E.tensor_mul(
                        dBu_t[:], dtu[:], brep[:, n * CH:(n + 1) * CH]),
                        reads=("dtu", "brep"), writes=("dBu_t",))
                    b.op("vector", lambda E, dg=dg, n=n: E.tensor_tensor_scan(
                        h_t[:], dA_t[:], dBu_t[:],
                        hcarry[:, dg * 16 + n:dg * 16 + n + 1],
                        ALU.mult, ALU.add),
                        reads=("dA_t", "dBu_t", "hcarry"), writes=("h_t",))
                    b.op("vector", lambda E, dg=dg, n=n: E.tensor_copy(
                        hcarry[:, dg * 16 + n:dg * 16 + n + 1],
                        h_t[:, CH - 1:CH]),
                        reads=("h_t",), writes=("hcarry",))
                    b.op("vector", lambda E, n=n: E.tensor_mul(
                        hc_t[:], h_t[:], crep[:, n * CH:(n + 1) * CH]),
                        reads=("h_t", "crep"), writes=("hc_t",))
                    b.op("vector", lambda E: E.tensor_add(
                        yacc[:], yacc[:], hc_t[:]),
                        reads=("hc_t", "yacc"), writes=("yacc",))
                b.op("vector", lambda E, dg=dg: E.scalar_tensor_tensor(
                    yacc[:], xc[:, dg * CH:(dg + 1) * CH], Dv[:, dg:dg + 1],
                    yacc[:], ALU.mult, ALU.add),
                    reads=("xc", "Dv", "yacc"), writes=("yacc",))
                b.op("vector", lambda E, dg=dg: E.tensor_mul(
                    y3[:, dg * CH:(dg + 1) * CH], yacc[:],
                    siluz[:, dg * CH:(dg + 1) * CH]),
                    reads=("yacc", "siluz"), writes=("y3",))
            for j in range(8):
                for half in range(2):
                    k = j * 2 + half
                    b.dma("gpsimd", wp[j][:, half * 1024:half * 1024 + 1024],
                          wop_in[k * 128:(k + 1) * 128, :], writes=(f"wp{j}",))
            for og in range(8):
                for k in range(16):
                    b.op("tensor", lambda E, og=og, k=k: E.matmul(
                        psA[:], wp[k // 2][:, (k % 2) * 1024 + og * 128:
                                           (k % 2) * 1024 + (og + 1) * 128],
                        y3[:, k * CH:(k + 1) * CH],
                        start=(k == 0), stop=(k == 15)),
                        reads=(f"wp{k // 2}", "y3"),
                        writes=("psA",) if k in (0, 15) else (),
                        inc=1 if k in (0, 15) else 0)
                b.op("scalar", lambda E, og=og: E.activation(
                    mT[:, og * CH:(og + 1) * CH], psA[:], AF.Gelu),
                    reads=("psA",), writes=("mT",))
            for k in range(8):
                b.dma("gpsimd", wp[k][:], wglu_in[k * 128:(k + 1) * 128, :],
                      writes=(f"wp{k}",))
            for og in range(16):
                for k in range(8):
                    b.op("tensor", lambda E, og=og, k=k: E.matmul(
                        psA[:], wp[k][:, og * 128:(og + 1) * 128],
                        mT[:, k * CH:(k + 1) * CH],
                        start=(k == 0), stop=(k == 7)),
                        reads=(f"wp{k}", "mT"),
                        writes=("psA",) if k in (0, 7) else (),
                        inc=1 if k in (0, 7) else 0)
                if og < 8:
                    b.op("scalar", lambda E, og=og: E.activation(
                        g1[:, og * CH:(og + 1) * CH], psA[:], AF.Copy),
                        reads=("psA",), writes=("g1",))
                    b.op("vector", lambda E, og=og: E.tensor_scalar(
                        g1[:, og * CH:(og + 1) * CH],
                        g1[:, og * CH:(og + 1) * CH],
                        glub[:, og:og + 1], None, ALU.add),
                        reads=("g1", "glub"), writes=("g1",))
                else:
                    b.op("scalar", lambda E, og=og: E.activation(
                        sg2[:, (og - 8) * CH:(og - 7) * CH], psA[:], AF.Sigmoid,
                        bias=glub[:, og:og + 1]),
                        reads=("psA", "glub"), writes=("sg2",))
            b.dma("sync", xn2[:].rearrange("p (g t) -> p g t", g=8),
                  xnb_hbm[:, :, c0:c0 + CH], writes=("xn2",))
            for g in range(8):
                b.op("vector", lambda E, g=g: E.tensor_mul(
                    s_t[:, g * CH:(g + 1) * CH], g1[:, g * CH:(g + 1) * CH],
                    sg2[:, g * CH:(g + 1) * CH]),
                    reads=("g1", "sg2"), writes=("s_t",))
                b.op("vector", lambda E, g=g: E.tensor_add(
                    s_t[:, g * CH:(g + 1) * CH], s_t[:, g * CH:(g + 1) * CH],
                    xn2[:, g * CH:(g + 1) * CH]),
                    reads=("s_t", "xn2"), writes=("s_t",))
            b.dma("sync", sT_hbm[:, :, c0:c0 + CH],
                  s_t[:].rearrange("p (g t) -> p g t", g=8), reads=("s_t",))
        b.finish_into(st)
    return nc




def build_C():
    nc = bass.Bass("TRN2", target_bir_lowering=False, debug=False)
    ap = lambda name, shape, dt_, kind: nc.dram_tensor(name, shape, dt_, kind=kind).ap()
    xnb_in = nc.dram_tensor("xnT_b", [D, T], BF16, kind="ExternalInput")
    xnb_hbm = xnb_in.ap().rearrange("(g p) t -> p g t", p=128)
    route_in = ap("route", [T, 20], F32, "ExternalInput")
    ac_in = ap("all_counts", [E0, E0], F32, "ExternalInput")
    pm_in = ap("prevmask", [E0, 1], F32, "ExternalInput")
    w1_in = ap("w1", [E0, D, 4096], F32, "ExternalInput")
    b1_in = ap("b1P", [E0, 128, 32], F32, "ExternalInput")
    w2_in = ap("w2", [E0, 4096, D], F32, "ExternalInput")
    b2_in = ap("b2", [E0, D], F32, "ExternalInput")
    sel8_in = ap("sel8", [E0, E0 * 128], F32, "ExternalInput")
    eye_in = ap("eye", [128, 128], F32, "ExternalInput")
    ones_in = ap("ones", [128, 128], F32, "ExternalInput")
    gT_out = nc.dram_tensor("gT", [D, T], BF16, kind="ExternalOutput").ap()

    b = B(nc)
    NT = T // 128
    with contextlib.ExitStack() as st:
        e = st.enter_context
        wp = [e(nc.sbuf_tensor(f"wp{i}", [128, 2048], BF16)) for i in range(8)]
        xnb = e(nc.sbuf_tensor("xnb", [128, 8 * T], BF16))
        hfb = e(nc.sbuf_tensor("hfb", [128, 16 * T], BF16))
        yacc = e(nc.sbuf_tensor("yacc", [128, 8 * T], BF16))
        eye = e(nc.sbuf_tensor("sb_eye", [128, 128], F32))
        ones = e(nc.sbuf_tensor("sb_ones", [128, 128], F32))
        sel8 = e(nc.sbuf_tensor("sb_sel8", [E0, E0 * 128], F32))
        b1t = e(nc.sbuf_tensor("b1t", [128, 32], F32))
        b2t = e(nc.sbuf_tensor("b2t", [E0, D], F32))
        acs = e(nc.sbuf_tensor("acs", [E0, E0], F32))
        pms = e(nc.sbuf_tensor("pms", [E0, 1], F32))
        offs_rep = e(nc.sbuf_tensor("offs_rep", [128, E0], F32))
        rt = e(nc.sbuf_tensor("rt", [128, 20], F32))
        mtok = e(nc.sbuf_tensor("mtok", [128, E0], F32))
        scr = e(nc.sbuf_tensor("scr", [128, E0], F32))
        wcol = e(nc.sbuf_tensor("wcol", [128, 4], F32))
        mTt = e(nc.sbuf_tensor("mTt", [E0, T], F32))
        wnT = e(nc.sbuf_tensor("wnT", [1, T], F32))
        mrep = e(nc.sbuf_tensor("mrep", [128, T], BF16))
        wnrep = e(nc.sbuf_tensor("wnrep", [128, 512], BF16))
        gtmp = e(nc.sbuf_tensor("gtmp", [128, 8 * 512], BF16))
        ps8 = [e(nc.psum_tensor(f"ps8_{i}", [128, 512], F32)) for i in range(8)]

        b.dma("sync", eye[:], eye_in, writes=("eye",))
        b.dma("sync", ones[:], ones_in, writes=("ones",))
        b.dma("sync", sel8[:], sel8_in, writes=("sel8",))
        b.dma("sync", b2t[:], b2_in, writes=("b2t",))
        b.dma("sync", acs[:], ac_in, writes=("acs",))
        b.dma("sync", pms[:], pm_in, writes=("pms",))
        b.op("tensor", lambda E: E.matmul(ps8[0][0:1, 0:E0], pms[:], acs[:],
                                          start=True, stop=True),
             reads=("pms", "acs"), writes=("ps0",))
        b.op("scalar", lambda E: E.activation(offs_rep[0:1, :], ps8[0][0:1, 0:E0],
                                              AF.Copy),
             reads=("ps0",), writes=("offs_row",))
        b.op("tensor", lambda E: E.matmul(ps8[1][:, 0:E0], ones[0:1, :],
                                          offs_rep[0:1, :], start=True, stop=True),
             reads=("ones", "offs_row"), writes=("ps1",))
        b.op("scalar", lambda E: E.activation(offs_rep[:], ps8[1][:, 0:E0], AF.Copy),
             reads=("ps1",), writes=("offs_rep",))
        for i in range(NT):
            b.dma("sync", rt[:], route_in[i * 128:(i + 1) * 128, :], writes=("rt",))
            b.op("vector", lambda E: E.tensor_mul(scr[:], offs_rep[:], rt[:, 0:8]),
                 reads=("offs_rep", "rt"), writes=("scr",))
            b.op("vector", lambda E: E.tensor_reduce(
                wcol[:, 0:1], scr[:], axis=AX.X, op=ALU.add),
                reads=("scr",), writes=("wcol",))
            b.op("vector", lambda E: E.tensor_add(wcol[:, 0:1], wcol[:, 0:1],
                                                  rt[:, 18:19]),
                 reads=("wcol", "rt"), writes=("wcol",))
            b.op("vector", lambda E: E.tensor_scalar(
                wcol[:, 0:1], wcol[:, 0:1], float(CAP), None, ALU.is_lt),
                reads=("wcol",), writes=("wcol",))
            b.op("vector", lambda E: E.tensor_mul(wcol[:, 0:1], wcol[:, 0:1],
                                                  rt[:, 16:17]),
                 reads=("wcol", "rt"), writes=("wcol",))
            b.op("vector", lambda E: E.tensor_mul(scr[:], offs_rep[:], rt[:, 8:16]),
                 reads=("offs_rep", "rt"), writes=("scr",))
            b.op("vector", lambda E: E.tensor_reduce(
                wcol[:, 1:2], scr[:], axis=AX.X, op=ALU.add),
                reads=("scr",), writes=("wcol",))
            b.op("vector", lambda E: E.tensor_add(wcol[:, 1:2], wcol[:, 1:2],
                                                  rt[:, 19:20]),
                 reads=("wcol", "rt"), writes=("wcol",))
            b.op("vector", lambda E: E.tensor_scalar(
                wcol[:, 1:2], wcol[:, 1:2], float(CAP), None, ALU.is_lt),
                reads=("wcol",), writes=("wcol",))
            b.op("vector", lambda E: E.tensor_mul(wcol[:, 1:2], wcol[:, 1:2],
                                                  rt[:, 17:18]),
                 reads=("wcol", "rt"), writes=("wcol",))
            b.op("vector", lambda E: E.tensor_scalar(
                mtok[:], rt[:, 0:8], wcol[:, 0:1], None, ALU.mult),
                reads=("rt", "wcol"), writes=("mtok",))
            b.op("vector", lambda E: E.tensor_scalar(
                scr[:], rt[:, 8:16], wcol[:, 1:2], None, ALU.mult),
                reads=("rt", "wcol"), writes=("scr",))
            b.op("vector", lambda E: E.tensor_add(mtok[:], mtok[:], scr[:]),
                 reads=("mtok", "scr"), writes=("mtok",))
            b.op("vector", lambda E: E.tensor_add(wcol[:, 2:3], wcol[:, 0:1],
                                                  wcol[:, 1:2]),
                 reads=("wcol",), writes=("wcol",))
            b.op("vector", lambda E: E.tensor_scalar(
                wcol[:, 2:3], wcol[:, 2:3], 1e-6, None, ALU.max),
                reads=("wcol",), writes=("wcol",))
            b.op("vector", lambda E: E.reciprocal(wcol[:, 3:4], wcol[:, 2:3]),
                 reads=("wcol",), writes=("wcol",))
            b.op("tensor", lambda E: E.matmul(ps8[2][0:8, 0:128], mtok[:], eye[:],
                                              start=True, stop=True),
                 reads=("mtok", "eye"), writes=("ps2",))
            b.op("scalar", lambda E, i=i: E.activation(
                mTt[:, i * 128:(i + 1) * 128], ps8[2][0:8, 0:128], AF.Copy),
                reads=("ps2",), writes=("mTt",))
            b.op("tensor", lambda E: E.matmul(ps8[3][0:1, 0:128], wcol[:, 3:4],
                                              eye[:], start=True, stop=True),
                 reads=("wcol", "eye"), writes=("ps3",))
            b.op("scalar", lambda E, i=i: E.activation(
                wnT[:, i * 128:(i + 1) * 128], ps8[3][0:1, 0:128], AF.Copy),
                reads=("ps3",), writes=("wnT",))
        b.dma("sync", xnb[:].rearrange("p (g t) -> p g t", g=8),
              xnb_hbm[:, :, :], writes=("xnb",))
        b.op("vector", lambda E: E.memset(yacc[:], 0.0), writes=("yacc",))
        for ex in range(E0):
            b.dma("sync", b1t[:], b1_in[ex], writes=("b1t",))
            for nt in range(4):
                b.op("tensor", lambda E, ex=ex, nt=nt: E.matmul(
                    ps8[4][:], sel8[:, ex * 128:(ex + 1) * 128],
                    mTt[:, nt * 512:(nt + 1) * 512], start=True, stop=True),
                    reads=("sel8", "mTt"), writes=("ps4",))
                b.op("scalar", lambda E, nt=nt: E.activation(
                    mrep[:, nt * 512:(nt + 1) * 512], ps8[4][:], AF.Copy),
                    reads=("ps4",), writes=("mrep",))
            for fb in range(2):
                for k in range(8):
                    b.dma("gpsimd", wp[k][:],
                          w1_in[ex, k * 128:(k + 1) * 128,
                                fb * 2048:fb * 2048 + 2048],
                          writes=(f"wp{k}",))
                for ogl in range(16):
                    og = fb * 16 + ogl
                    for nt in range(4):
                        for k in range(8):
                            b.op("tensor", lambda E, ogl=ogl, nt=nt, k=k: E.matmul(
                                ps8[0][:], wp[k][:, ogl * 128:(ogl + 1) * 128],
                                xnb[:, k * T + nt * 512:k * T + (nt + 1) * 512],
                                start=(k == 0), stop=(k == 7)),
                                reads=(f"wp{k}", "xnb"),
                                writes=("ps0",) if k in (0, 7) else (),
                                inc=1 if k in (0, 7) else 0)
                        b.op("scalar", lambda E, og=og, ogl=ogl, nt=nt: E.activation(
                            hfb[:, ogl * T + nt * 512:ogl * T + (nt + 1) * 512],
                            ps8[0][:], AF.Gelu, bias=b1t[:, og:og + 1]),
                            reads=("ps0", "b1t"), writes=("hfb",))
                    b.op("vector", lambda E, ogl=ogl: E.tensor_mul(
                        hfb[:, ogl * T:(ogl + 1) * T],
                        hfb[:, ogl * T:(ogl + 1) * T], mrep[:]),
                        reads=("hfb", "mrep"), writes=("hfb",))
                for k2 in range(16):
                    b.dma("gpsimd", wp[k2 // 2][:, (k2 % 2) * 1024:(k2 % 2 + 1) * 1024],
                          w2_in[ex, fb * 2048 + k2 * 128:fb * 2048 + (k2 + 1) * 128, :],
                          writes=(f"wp{k2 // 2}",))
                for nt in range(4):
                    for k2 in range(16):
                        for og in range(8):
                            b.op("tensor", lambda E, k2=k2, og=og, nt=nt: E.matmul(
                                ps8[og][:],
                                wp[k2 // 2][:, (k2 % 2) * 1024 + og * 128:
                                            (k2 % 2) * 1024 + (og + 1) * 128],
                                hfb[:, k2 * T + nt * 512:k2 * T + (nt + 1) * 512],
                                start=(k2 == 0), stop=(k2 == 15)),
                                reads=(f"wp{k2 // 2}", "hfb"),
                                writes=(f"ps{og}",) if k2 in (0, 15) else (),
                                inc=1 if k2 in (0, 15) else 0)
                    for og in range(8):
                        b.op("vector", lambda E, og=og, nt=nt: E.scalar_tensor_tensor(
                            yacc[:, og * T + nt * 512:og * T + (nt + 1) * 512],
                            ps8[og][:], 1.0,
                            yacc[:, og * T + nt * 512:og * T + (nt + 1) * 512],
                            ALU.mult, ALU.add),
                            reads=(f"ps{og}", "yacc"), writes=("yacc",))
        for og in range(8):
            for nt in range(4):
                b.op("tensor", lambda E, og=og, nt=nt: E.matmul(
                    ps8[og][:], b2t[:, og * 128:(og + 1) * 128],
                    mTt[:, nt * 512:(nt + 1) * 512], start=True, stop=True),
                    reads=("b2t", "mTt"), writes=(f"ps{og}",))
                b.op("vector", lambda E, og=og, nt=nt: E.scalar_tensor_tensor(
                    yacc[:, og * T + nt * 512:og * T + (nt + 1) * 512],
                    ps8[og][:], 1.0,
                    yacc[:, og * T + nt * 512:og * T + (nt + 1) * 512],
                    ALU.mult, ALU.add),
                    reads=(f"ps{og}", "yacc"), writes=("yacc",))
        for nt in range(4):
            b.op("tensor", lambda E, nt=nt: E.matmul(
                ps8[4][:], ones[0:1, :], wnT[:, nt * 512:(nt + 1) * 512],
                start=True, stop=True),
                reads=("ones", "wnT"), writes=("ps4",))
            b.op("scalar", lambda E: E.activation(wnrep[:], ps8[4][:], AF.Copy),
                 reads=("ps4",), writes=("wnrep",))
            for og in range(8):
                b.op("vector", lambda E, og=og, nt=nt: E.tensor_mul(
                    gtmp[:, og * 512:(og + 1) * 512],
                    yacc[:, og * T + nt * 512:og * T + (nt + 1) * 512], wnrep[:]),
                    reads=("yacc", "wnrep"), writes=("gtmp",))
                b.op("scalar", lambda E, og=og: E.activation(
                    gtmp[:, og * 512:(og + 1) * 512],
                    gtmp[:, og * 512:(og + 1) * 512], AF.Silu),
                    reads=("gtmp",), writes=("gtmp",))
            b.dma("sync", gT_out.rearrange("(g p) t -> p g t", p=128)[:, :, nt * 512:(nt + 1) * 512],
                  gtmp[:, 0:8 * 512].rearrange("p (g t) -> p g t", g=8),
                  reads=("gtmp",))
        b.finish_into(st)
    return nc



def build_D():
    nc = bass.Bass("TRN2", target_bir_lowering=False, debug=False)
    x_in = nc.dram_tensor("x", [T, D], F32, kind="ExternalInput").ap()
    sT_in = nc.dram_tensor("sT", [D, T], BF16, kind="ExternalInput")
    gT_in = nc.dram_tensor("gT", [D, T], BF16, kind="ExternalInput")
    eye_in = nc.dram_tensor("eye", [128, 128], F32, kind="ExternalInput").ap()
    ones_in = nc.dram_tensor("ones", [128, 128], F32, kind="ExternalInput").ap()
    sT_hbm = sT_in.ap().rearrange("(g p) t -> p g t", p=128)
    gT_hbm = gT_in.ap().rearrange("(g p) t -> p g t", p=128)
    x1_out = nc.dram_tensor("x1T", [D, T], F32, kind="ExternalOutput")
    x1n_out = nc.dram_tensor("x1nT_b", [D, T], BF16, kind="ExternalOutput")
    x1_hbm = x1_out.ap().rearrange("(g p) t -> p g t", p=128)
    x1n_hbm = x1n_out.ap().rearrange("(g p) t -> p g t", p=128)

    b = B(nc)
    CH = 512
    with contextlib.ExitStack() as st:
        e = st.enter_context
        eye = e(nc.sbuf_tensor("sb_eye", [128, 128], F32))
        ones = e(nc.sbuf_tensor("sb_ones", [128, 128], F32))
        xt = e(nc.sbuf_tensor("xt", [128, D], F32))
        xT = e(nc.sbuf_tensor("xT", [128, 8 * CH], F32))
        s_c = e(nc.sbuf_tensor("s_c", [128, 8 * CH], BF16))
        g_c = e(nc.sbuf_tensor("g_c", [128, 8 * CH], BF16))
        x1c = e(nc.sbuf_tensor("x1c", [128, 8 * CH], F32))
        xsq = e(nc.sbuf_tensor("xsq", [128, 8 * CH], F32))
        strow = e(nc.sbuf_tensor("strow", [1, 6 * CH], F32))
        rrep = e(nc.sbuf_tensor("rrep", [128, CH], F32))
        mrep2 = e(nc.sbuf_tensor("mrep2", [128, CH], F32))
        cvals = e(nc.sbuf_tensor("cvals", [128, 1], F32))
        ps_t = e(nc.psum_tensor("ps_t", [128, 512], F32))
        ps_s = e(nc.psum_tensor("ps_s", [1, 512], F32))
        ps_q = e(nc.psum_tensor("ps_q", [1, 512], F32))
        ps_b1 = e(nc.psum_tensor("ps_b1", [128, 512], F32))
        ps_b2 = e(nc.psum_tensor("ps_b2", [128, 512], F32))

        b.dma("sync", eye[:], eye_in, writes=("eye",))
        b.dma("sync", ones[:], ones_in, writes=("ones",))
        b.op("vector", lambda E: E.memset(cvals[:], 1e-5), writes=("cvals",))
        for ci in range(4):
            c0 = ci * CH
            b.dma("sync", s_c[:].rearrange("p (g t) -> p g t", g=8),
                  sT_hbm[:, :, c0:c0 + CH], writes=("s_c",))
            b.dma("sync", g_c[:].rearrange("p (g t) -> p g t", g=8),
                  gT_hbm[:, :, c0:c0 + CH], writes=("g_c",))
            for tt in range(4):
                b.dma("sync", xt[:], x_in[c0 + tt * 128:c0 + (tt + 1) * 128, :],
                      writes=("xt",))
                for jj in (0, 4):
                    for j in range(jj, jj + 4):
                        b.op("tensor", lambda E, j=j, jj=jj: E.transpose(
                            ps_t[:, (j - jj) * 128:(j - jj) * 128 + 128],
                            xt[:, j * 128:(j + 1) * 128], eye[:]),
                            reads=("xt", "eye"), writes=("ps_t",))
                    for j in range(jj, jj + 4):
                        b.op("scalar", lambda E, j=j, jj=jj, tt=tt: E.activation(
                            xT[:, j * CH + tt * 128:j * CH + (tt + 1) * 128],
                            ps_t[:, (j - jj) * 128:(j - jj) * 128 + 128], AF.Copy),
                            reads=("ps_t",), writes=("xT",))
            for g in range(8):
                b.op("vector", lambda E, g=g: E.tensor_mul(
                    x1c[:, g * CH:(g + 1) * CH], s_c[:, g * CH:(g + 1) * CH],
                    g_c[:, g * CH:(g + 1) * CH]),
                    reads=("s_c", "g_c"), writes=("x1c",))
                b.op("vector", lambda E, g=g: E.tensor_add(
                    x1c[:, g * CH:(g + 1) * CH], x1c[:, g * CH:(g + 1) * CH],
                    xT[:, g * CH:(g + 1) * CH]),
                    reads=("x1c", "xT"), writes=("x1c",))
            b.dma("sync", x1_hbm[:, :, c0:c0 + CH],
                  x1c[:].rearrange("p (g t) -> p g t", g=8), reads=("x1c",))
            b.op("scalar", lambda E: E.activation(xsq[:], x1c[:], AF.Square),
                 reads=("x1c",), writes=("xsq",))
            for g in range(8):
                b.op("tensor", lambda E, g=g: E.matmul(
                    ps_s[:], ones[:, 0:1], x1c[:, g * CH:(g + 1) * CH],
                    start=(g == 0), stop=(g == 7)),
                    reads=("ones", "x1c"),
                    writes=("ps_s",) if g in (0, 7) else (),
                    inc=1 if g in (0, 7) else 0)
            for g in range(8):
                b.op("tensor", lambda E, g=g: E.matmul(
                    ps_q[:], ones[:, 0:1], xsq[:, g * CH:(g + 1) * CH],
                    start=(g == 0), stop=(g == 7)),
                    reads=("ones", "xsq"),
                    writes=("ps_q",) if g in (0, 7) else (),
                    inc=1 if g in (0, 7) else 0)
            b.op("scalar", lambda E: E.activation(
                strow[:, 0 * CH:1 * CH], ps_s[:], AF.Copy, scale=1.0 / D),
                reads=("ps_s",), writes=("strow",))
            b.op("scalar", lambda E: E.activation(
                strow[:, 1 * CH:2 * CH], ps_q[:], AF.Copy, scale=1.0 / D),
                reads=("ps_q",), writes=("strow",))
            b.op("vector", lambda E: E.tensor_mul(
                strow[:, 2 * CH:3 * CH], strow[:, 0:CH], strow[:, 0:CH]),
                reads=("strow",), writes=("strow",))
            b.op("vector", lambda E: E.tensor_sub(
                strow[:, 2 * CH:3 * CH], strow[:, CH:2 * CH], strow[:, 2 * CH:3 * CH]),
                reads=("strow",), writes=("strow",))
            b.op("scalar", lambda E: E.activation(
                strow[:, 3 * CH:4 * CH], strow[:, 2 * CH:3 * CH], AF.Ln,
                bias=cvals[0:1, 0:1]),
                reads=("strow", "cvals"), writes=("strow",))
            b.op("scalar", lambda E: E.activation(
                strow[:, 3 * CH:4 * CH], strow[:, 3 * CH:4 * CH], AF.Exp, scale=-0.5),
                reads=("strow",), writes=("strow",))
            b.op("vector", lambda E: E.tensor_mul(
                strow[:, 4 * CH:5 * CH], strow[:, 0:CH], strow[:, 3 * CH:4 * CH]),
                reads=("strow",), writes=("strow",))
            b.op("vector", lambda E: E.tensor_scalar(
                strow[:, 4 * CH:5 * CH], strow[:, 4 * CH:5 * CH], -1.0, None,
                ALU.mult),
                reads=("strow",), writes=("strow",))
            b.op("tensor", lambda E: E.matmul(
                ps_b1[:], ones[0:1, :], strow[:, 3 * CH:4 * CH],
                start=True, stop=True),
                reads=("ones", "strow"), writes=("ps_b1",))
            b.op("scalar", lambda E: E.activation(rrep[:], ps_b1[:], AF.Copy),
                 reads=("ps_b1",), writes=("rrep",))
            b.op("tensor", lambda E: E.matmul(
                ps_b2[:], ones[0:1, :], strow[:, 4 * CH:5 * CH],
                start=True, stop=True),
                reads=("ones", "strow"), writes=("ps_b2",))
            b.op("scalar", lambda E: E.activation(mrep2[:], ps_b2[:], AF.Copy),
                 reads=("ps_b2",), writes=("mrep2",))
            for g in range(8):
                b.op("vector", lambda E, g=g: E.tensor_mul(
                    xsq[:, g * CH:(g + 1) * CH], x1c[:, g * CH:(g + 1) * CH],
                    rrep[:]),
                    reads=("x1c", "rrep"), writes=("xsq",))
                b.op("vector", lambda E, g=g: E.tensor_add(
                    g_c[:, g * CH:(g + 1) * CH], xsq[:, g * CH:(g + 1) * CH],
                    mrep2[:]),
                    reads=("xsq", "mrep2"), writes=("g_c",))
            b.dma("sync", x1n_hbm[:, :, c0:c0 + CH],
                  g_c[:].rearrange("p (g t) -> p g t", g=8), reads=("g_c",))
        b.finish_into(st)
    return nc




def build_E():
    nc = bass.Bass("TRN2", target_bir_lowering=False, debug=False)
    x1n_in = nc.dram_tensor("x1nT_b", [D, T], BF16, kind="ExternalInput")
    x1n_hbm = x1n_in.ap().rearrange("(g p) t -> p g t", p=128)
    qkw_in = nc.dram_tensor("qk_w", [D, 2 * D], F32, kind="ExternalInput").ap()
    qkb_in = nc.dram_tensor("qk_b", [2 * D], F32, kind="ExternalInput").ap()
    lepe_in = nc.dram_tensor("lepe_w", [3, D, D], F32, kind="ExternalInput").ap()
    lepeb_in = nc.dram_tensor("lepebP", [128, 8], F32, kind="ExternalInput").ap()
    eye_in = nc.dram_tensor("eye", [128, 128], F32, kind="ExternalInput").ap()
    ones_in = nc.dram_tensor("ones", [128, 128], F32, kind="ExternalInput").ap()
    sel8_in = nc.dram_tensor("sel8", [E0, E0 * 128], F32, kind="ExternalInput").ap()
    s1_out = nc.dram_tensor("s1T", [D, T], BF16, kind="ExternalOutput")
    s1_hbm = s1_out.ap().rearrange("(g p) t -> p g t", p=128)

    b = B(nc)
    NT = T // 128
    SC = 1.0 / np.sqrt(2048.0)
    with contextlib.ExitStack() as st:
        e = st.enter_context
        x1n = e(nc.sbuf_tensor("x1n", [128, 8 * T], BF16))
        s1 = e(nc.sbuf_tensor("s1", [128, 8 * T], BF16))
        wpool = e(nc.sbuf_tensor("wpool", [128, 24 * 1024], BF16))
        eye = e(nc.sbuf_tensor("sb_eye", [128, 128], F32))
        eyeb = e(nc.sbuf_tensor("sb_eyeb", [128, 128], BF16))
        ones = e(nc.sbuf_tensor("sb_ones", [128, 128], F32))
        onesb = e(nc.sbuf_tensor("sb_onesb", [128, 128], BF16))
        sel8 = e(nc.sbuf_tensor("sb_sel8", [E0, E0 * 128], F32))
        lepeb = e(nc.sbuf_tensor("lepeb", [128, 8], F32))
        qkbr = e(nc.sbuf_tensor("qkbr", [1, 2048], F32))
        qkbrep = e(nc.sbuf_tensor("qkbrep", [128, 2048], F32))
        threp = e(nc.sbuf_tensor("threp", [128, 512], F32))
        tcol = e(nc.sbuf_tensor("tcol", [128, 1], F32))
        ang = e(nc.sbuf_tensor("ang", [128, 512], F32))
        ang2 = e(nc.sbuf_tensor("ang2", [128, 512], F32))
        angi = e(nc.sbuf_tensor("angi", [128, 512], I32))
        sint = e(nc.sbuf_tensor("sint", [128, 512], BF16))
        cost = e(nc.sbuf_tensor("cost", [128, 512], BF16))
        kqf = e(nc.sbuf_tensor("kqf", [128, 1024], F32))
        sca = e(nc.sbuf_tensor("sca", [128, 1024], F32))
        kq = e(nc.sbuf_tensor("kq", [128, 1024], BF16))
        kqr = e(nc.sbuf_tensor("kqr", [128, 1024], BF16))
        v_t = e(nc.sbuf_tensor("v_t", [128, 1024], BF16))
        qrT = e(nc.sbuf_tensor("qrT", [128, 1024], BF16))
        kv_sb = e(nc.sbuf_tensor("kv_sb", [128, 1024], BF16))
        kmean = e(nc.sbuf_tensor("kmean", [1, 1024], F32))
        kmrep = e(nc.sbuf_tensor("kmrep", [128, 1024], BF16))
        zden = e(nc.sbuf_tensor("zden", [128, 8], F32))
        zT = e(nc.sbuf_tensor("zT", [8, T], F32))
        zrep = e(nc.sbuf_tensor("zrep", [128, 512], BF16))
        otmp = e(nc.sbuf_tensor("otmp", [128, 1024], BF16))
        ps_a = e(nc.psum_tensor("ps_a", [128, 512], F32))
        ps_trb = e(nc.psum_tensor("ps_trb", [128, 512], BF16))
        ps_kv = e(nc.psum_tensor("ps_kv", [128, 1024], F32))
        ps_km = e(nc.psum_tensor("ps_km", [1, 1024], F32))
        ps_o = e(nc.psum_tensor("ps_o", [128, 1024], F32))

        b.dma("sync", eye[:], eye_in, writes=("eye",))
        b.dma("sync", ones[:], ones_in, writes=("ones",))
        b.dma("gpsimd", eyeb[:], eye_in, writes=("eyeb",))
        b.dma("gpsimd", onesb[:], ones_in, writes=("onesb",))
        b.dma("sync", sel8[:], sel8_in, writes=("sel8",))
        b.dma("sync", lepeb[:], lepeb_in, writes=("lepeb",))
        b.dma("sync", qkbr[:], qkb_in.rearrange("(o t) -> o t", o=1), writes=("qkbr",))
        b.dma("sync", x1n[:].rearrange("p (g t) -> p g t", g=8),
              x1n_hbm[:, :, :], writes=("x1n",))
        for nt in range(4):
            b.op("tensor", lambda E, nt=nt: E.matmul(
                ps_a[:], ones[0:1, :], qkbr[:, nt * 512:(nt + 1) * 512],
                start=True, stop=True),
                reads=("ones", "qkbr"), writes=("ps_a",))
            b.op("scalar", lambda E, nt=nt: E.activation(
                qkbrep[:, nt * 512:(nt + 1) * 512], ps_a[:], AF.Copy),
                reads=("ps_a",), writes=("qkbrep",))
        b.op("gpsimd", lambda E: E.iota(ang[0:1, :], [[1, 512]],
                                        channel_multiplier=0,
                                        allow_small_or_imprecise_dtypes=True),
             writes=("ang",))
        b.op("scalar", lambda E: E.activation(
            ang[0:1, :], ang[0:1, :], AF.Exp, scale=float(-np.log(10000.0) / 512)),
            reads=("ang",), writes=("ang",))
        b.op("tensor", lambda E: E.matmul(ps_a[:], ones[0:1, :], ang[0:1, :],
                                          start=True, stop=True),
             reads=("ones", "ang"), writes=("ps_a",))
        b.op("scalar", lambda E: E.activation(threp[:], ps_a[:], AF.Copy),
             reads=("ps_a",), writes=("threp",))

        for w in range(3):
            for k in range(8):
                b.dma("gpsimd", wpool[:, (w * 8 + k) * 1024:(w * 8 + k + 1) * 1024],
                      lepe_in[w, k * 128:(k + 1) * 128, :], writes=("wpool",))
        for og in range(8):
            for nt in range(4):
                first = True
                for w in range(3):
                    lo = nt * 512 + (w - 1)
                    hi = lo + 512
                    plo, phi = 0, 512
                    if lo < 0:
                        plo = 1; lo = 0
                    if hi > T:
                        phi = 511; hi = T
                    for k in range(8):
                        last = (w == 2 and k == 7)
                        b.op("tensor", lambda E, og=og, w=w, k=k, lo=lo, hi=hi,
                             plo=plo, phi=phi, first=first: E.matmul(
                            ps_a[:, plo:phi],
                            wpool[:, (w * 8 + k) * 1024 + og * 128:
                                  (w * 8 + k) * 1024 + (og + 1) * 128],
                            x1n[:, k * T + lo:k * T + hi],
                            start=first, stop=(w == 2 and k == 7)),
                            reads=("wpool", "x1n"),
                            writes=("ps_a",) if (first or last) else (),
                            inc=1 if (first or last) else 0)
                        first = False
                b.op("scalar", lambda E, og=og, nt=nt: E.activation(
                    s1[:, og * T + nt * 512:og * T + (nt + 1) * 512],
                    ps_a[:], AF.Copy),
                    reads=("ps_a",), writes=("s1",))
                b.op("vector", lambda E, og=og, nt=nt: E.tensor_scalar(
                    s1[:, og * T + nt * 512:og * T + (nt + 1) * 512],
                    s1[:, og * T + nt * 512:og * T + (nt + 1) * 512],
                    lepeb[:, og:og + 1], None, ALU.add),
                    reads=("s1", "lepeb"), writes=("s1",))

        for k in range(8):
            for half in range(2):
                b.dma("gpsimd", wpool[:, (k * 2 + half) * 1024:(k * 2 + half + 1) * 1024],
                      qkw_in[k * 128:(k + 1) * 128, half * 1024:(half + 1) * 1024],
                      writes=("wpool",))

        def rope_tables(i):
            b.op("gpsimd", lambda E, i=i: E.iota(tcol[:], [[1, 1]], base=i * 128,
                                                 channel_multiplier=1,
                                                 allow_small_or_imprecise_dtypes=True),
                 writes=("tcol",))
            b.op("vector", lambda E: E.tensor_scalar(
                ang[:], threp[:], tcol[:, 0:1], None, ALU.mult),
                reads=("threp", "tcol"), writes=("ang",))
            for src, dst, dstk in ((ang, sint, "sint"), (ang, cost, "cost")):
                if dst is cost:
                    b.op("vector", lambda E: E.tensor_scalar(
                        ang[:], ang[:], float(np.pi / 2), None, ALU.add),
                        reads=("ang",), writes=("ang",))
                b.op("vector", lambda E: E.tensor_scalar(
                    ang2[:], ang[:], float(1.0 / TWOPI), None, ALU.mult),
                    reads=("ang",), writes=("ang2",))
                b.op("vector", lambda E: E.tensor_copy(angi[:], ang2[:]),
                     reads=("ang2",), writes=("angi",))
                b.op("vector", lambda E: E.tensor_copy(ang2[:], angi[:]),
                     reads=("angi",), writes=("ang2",))
                b.op("vector", lambda E: E.scalar_tensor_tensor(
                    ang2[:], ang2[:], float(-TWOPI), ang[:], ALU.mult, ALU.add),
                    reads=("ang2", "ang"), writes=("ang2",))
                b.op("vector", lambda E: E.tensor_scalar(
                    sca[:, 0:512], ang2[:], float(np.pi), None, ALU.is_gt),
                    reads=("ang2",), writes=("sca",))
                b.op("vector", lambda E: E.scalar_tensor_tensor(
                    ang2[:], sca[:, 0:512], float(-TWOPI), ang2[:], ALU.mult, ALU.add),
                    reads=("sca", "ang2"), writes=("ang2",))
                b.op("vector", lambda E: E.tensor_scalar(
                    sca[:, 0:512], ang2[:], float(-np.pi), None, ALU.is_lt),
                    reads=("ang2",), writes=("sca",))
                b.op("vector", lambda E: E.scalar_tensor_tensor(
                    ang2[:], sca[:, 0:512], float(TWOPI), ang2[:], ALU.mult, ALU.add),
                    reads=("sca", "ang2"), writes=("ang2",))
                b.op("scalar", lambda E, dst=dst: E.activation(
                    dst[:], ang2[:], AF.Sin),
                    reads=("ang2",), writes=(dstk,))

        def qk_stage(i, half):
            for nsub in range(2):
                for k in range(8):
                    b.op("tensor", lambda E, i=i, half=half, nsub=nsub, k=k: E.matmul(
                        ps_a[:], x1n[:, k * T + i * 128:k * T + (i + 1) * 128],
                        wpool[:, (k * 2 + half) * 1024 + nsub * 512:
                              (k * 2 + half) * 1024 + (nsub + 1) * 512],
                        start=(k == 0), stop=(k == 7)),
                        reads=("x1n", "wpool"),
                        writes=("ps_a",) if k in (0, 7) else (),
                        inc=1 if k in (0, 7) else 0)
                b.op("scalar", lambda E, nsub=nsub: E.activation(
                    kqf[:, nsub * 512:(nsub + 1) * 512], ps_a[:], AF.Copy),
                    reads=("ps_a",), writes=("kqf",))
            b.op("vector", lambda E, half=half: E.tensor_add(
                kqf[:], kqf[:], qkbrep[:, half * 1024:(half + 1) * 1024]),
                reads=("kqf", "qkbrep"), writes=("kqf",))
            b.op("vector", lambda E: E.tensor_scalar(
                sca[:], kqf[:], 0.0, None, ALU.min),
                reads=("kqf",), writes=("sca",))
            b.op("scalar", lambda E: E.activation(sca[:], sca[:], AF.Exp),
                 reads=("sca",), writes=("sca",))
            b.op("vector", lambda E: E.tensor_scalar(
                kqf[:], kqf[:], 0.0, None, ALU.max),
                reads=("kqf",), writes=("kqf",))
            b.op("vector", lambda E: E.tensor_add(kq[:], kqf[:], sca[:]),
                 reads=("kqf", "sca"), writes=("kq",))
            b.op("vector", lambda E: E.tensor_mul(
                kqr[:, 0:512], cost[:], kq[:, 0:1024:2]),
                reads=("cost", "kq"), writes=("kqr",))
            b.op("vector", lambda E: E.tensor_mul(
                sca[:, 0:512], sint[:], kq[:, 1:1024:2]),
                reads=("sint", "kq"), writes=("sca",))
            b.op("vector", lambda E: E.tensor_sub(
                kqr[:, 0:512], kqr[:, 0:512], sca[:, 0:512]),
                reads=("kqr", "sca"), writes=("kqr",))
            b.op("vector", lambda E: E.tensor_mul(
                kqr[:, 512:1024], cost[:], kq[:, 1:1024:2]),
                reads=("cost", "kq"), writes=("kqr",))
            b.op("vector", lambda E: E.tensor_mul(
                sca[:, 0:512], sint[:], kq[:, 0:1024:2]),
                reads=("sint", "kq"), writes=("sca",))
            b.op("vector", lambda E: E.tensor_add(
                kqr[:, 512:1024], kqr[:, 512:1024], sca[:, 0:512]),
                reads=("kqr", "sca"), writes=("kqr",))

        for i in range(NT):
            rope_tables(i)
            qk_stage(i, 1)
            for jj in (0, 4):
                for j in range(jj, jj + 4):
                    b.op("tensor", lambda E, i=i, j=j, jj=jj: E.transpose(
                        ps_trb[:, (j - jj) * 128:(j - jj) * 128 + 128],
                        x1n[:, j * T + i * 128:j * T + (i + 1) * 128], eyeb[:]),
                        reads=("x1n", "eyeb"), writes=("ps_trb",))
                for j in range(jj, jj + 4):
                    b.op("scalar", lambda E, j=j, jj=jj: E.activation(
                        v_t[:, j * 128:(j + 1) * 128],
                        ps_trb[:, (j - jj) * 128:(j - jj) * 128 + 128], AF.Copy),
                        reads=("ps_trb",), writes=("v_t",))
            for h in range(NH):
                b.op("tensor", lambda E, i=i, h=h: E.matmul(
                    ps_kv[:, h * 128:(h + 1) * 128], kqr[:, h * 128:(h + 1) * 128],
                    v_t[:, h * 128:(h + 1) * 128],
                    start=(i == 0 and h % 4 == 0),
                    stop=(i == NT - 1 and h % 4 == 3)),
                    reads=("kqr", "v_t"),
                    writes=("ps_kv",) if i in (0, NT - 1) else (),
                    inc=1 if i in (0, NT - 1) else 0)
            for half in range(2):
                b.op("tensor", lambda E, i=i, half=half: E.matmul(
                    ps_km[:, half * 512:(half + 1) * 512], onesb[:, 0:1],
                    kq[:, half * 512:(half + 1) * 512],
                    start=(i == 0), stop=(i == NT - 1)),
                    reads=("onesb", "kq"),
                    writes=("ps_km",) if i in (0, NT - 1) else (),
                    inc=1 if i in (0, NT - 1) else 0)
        b.op("scalar", lambda E: E.activation(kv_sb[:], ps_kv[:], AF.Copy,
                                              scale=float(SC)),
             reads=("ps_kv",), writes=("kv_sb",))
        b.op("scalar", lambda E: E.activation(kmean[:], ps_km[:], AF.Copy,
                                              scale=float(1.0 / T)),
             reads=("ps_km",), writes=("kmean",))
        for half in range(2):
            b.op("tensor", lambda E, half=half: E.matmul(
                ps_o[:, half * 512:(half + 1) * 512], ones[0:1, :],
                kmean[:, half * 512:(half + 1) * 512], start=True, stop=True),
                reads=("ones", "kmean"), writes=("ps_o",))
        b.op("scalar", lambda E: E.activation(kmrep[:], ps_o[:], AF.Copy),
             reads=("ps_o",), writes=("kmrep",))

        for i in range(NT):
            rope_tables(i)
            qk_stage(i, 0)
            b.op("vector", lambda E: E.tensor_mul(sca[:], kq[:], kmrep[:]),
                 reads=("kq", "kmrep"), writes=("sca",))
            b.op("vector", lambda E: E.tensor_reduce(
                zden[:], sca[:].rearrange("p (h d) -> p h d", h=8),
                axis=AX.X, op=ALU.add),
                reads=("sca",), writes=("zden",))
            b.op("vector", lambda E: E.tensor_scalar(
                zden[:], zden[:], 1e-6, None, ALU.add),
                reads=("zden",), writes=("zden",))
            b.op("vector", lambda E: E.reciprocal(zden[:], zden[:]),
                 reads=("zden",), writes=("zden",))
            b.op("tensor", lambda E: E.matmul(ps_a[0:8, 0:128], zden[:], eye[:],
                                              start=True, stop=True),
                 reads=("zden", "eye"), writes=("ps_a",))
            b.op("scalar", lambda E, i=i: E.activation(
                zT[:, i * 128:(i + 1) * 128], ps_a[0:8, 0:128], AF.Copy),
                reads=("ps_a",), writes=("zT",))
            for jj in (0, 4):
                for j in range(jj, jj + 4):
                    b.op("tensor", lambda E, j=j, jj=jj: E.transpose(
                        ps_trb[:, (j - jj) * 128:(j - jj) * 128 + 128],
                        kqr[:, j * 128:(j + 1) * 128], eyeb[:]),
                        reads=("kqr", "eyeb"), writes=("ps_trb",))
                for j in range(jj, jj + 4):
                    b.op("scalar", lambda E, j=j, jj=jj: E.activation(
                        qrT[:, j * 128:(j + 1) * 128],
                        ps_trb[:, (j - jj) * 128:(j - jj) * 128 + 128], AF.Copy),
                        reads=("ps_trb",), writes=("qrT",))
            for h in range(NH):
                b.op("tensor", lambda E, h=h: E.matmul(
                    ps_o[:, h * 128:(h + 1) * 128], kv_sb[:, h * 128:(h + 1) * 128],
                    qrT[:, h * 128:(h + 1) * 128],
                    start=(h % 4 == 0), stop=(h % 4 == 3)),
                    reads=("kv_sb", "qrT"), writes=("ps_o",))
            for hq in range(2):
                for h4 in range(4):
                    h = hq * 4 + h4
                    b.op("tensor", lambda E, h=h, h4=h4, i=i: E.matmul(
                        ps_a[:, h4 * 128:(h4 + 1) * 128],
                        sel8[:, h * 128:(h + 1) * 128],
                        zT[:, i * 128:(i + 1) * 128],
                        start=(h4 == 0), stop=(h4 == 3)),
                        reads=("sel8", "zT"), writes=("ps_a",))
                b.op("scalar", lambda E, hq=hq: E.activation(
                    zrep[:], ps_a[:], AF.Copy),
                    reads=("ps_a",), writes=("zrep",))
                for h4 in range(4):
                    h = hq * 4 + h4
                    b.op("vector", lambda E, h=h, h4=h4: E.tensor_mul(
                        otmp[:, h * 128:(h + 1) * 128],
                        ps_o[:, h * 128:(h + 1) * 128],
                        zrep[:, h4 * 128:(h4 + 1) * 128]),
                        reads=("ps_o", "zrep"), writes=("otmp",))
            for h in range(NH):
                b.op("vector", lambda E, h=h, i=i: E.tensor_add(
                    s1[:, h * T + i * 128:h * T + (i + 1) * 128],
                    s1[:, h * T + i * 128:h * T + (i + 1) * 128],
                    otmp[:, h * 128:(h + 1) * 128]),
                    reads=("s1", "otmp"), writes=("s1",))
        for nt in range(4):
            b.dma("sync", s1_hbm[:, :, nt * 512:(nt + 1) * 512],
                  s1[:, :].rearrange("p (g t) -> p g t", g=8)[:, :, nt * 512:(nt + 1) * 512],
                  reads=("s1",))
        b.finish_into(st)
    return nc




def build_F():
    nc = bass.Bass("TRN2", target_bir_lowering=False, debug=False)
    x1n_in = nc.dram_tensor("x1nT_b", [D, T], BF16, kind="ExternalInput")
    x1n_hbm = x1n_in.ap().rearrange("(g p) t -> p g t", p=128)
    x1_in = nc.dram_tensor("x1T", [D, T], F32, kind="ExternalInput")
    x1_hbm = x1_in.ap().rearrange("(g p) t -> p g t", p=128)
    s1_in = nc.dram_tensor("s1T", [D, T], BF16, kind="ExternalInput")
    s1_hbm = s1_in.ap().rearrange("(g p) t -> p g t", p=128)
    projP_in = nc.dram_tensor("projP", [128, 48], F32, kind="ExternalInput").ap()
    anchT_in = nc.dram_tensor("anchT", [6, 6], F32, kind="ExternalInput").ap()
    temp_in = nc.dram_tensor("rtemp", [1, 1], F32, kind="ExternalInput").ap()
    w1f_in = nc.dram_tensor("w1f", [E1, D, DE1], F32, kind="ExternalInput").ap()
    w2f_in = nc.dram_tensor("w2f", [E1, DE1, D], F32, kind="ExternalInput").ap()
    eye_in = nc.dram_tensor("eye", [128, 128], F32, kind="ExternalInput").ap()
    ones_in = nc.dram_tensor("ones", [128, 128], F32, kind="ExternalInput").ap()
    sel8_in = nc.dram_tensor("sel8", [E0, E0 * 128], F32, kind="ExternalInput").ap()
    out_hbm = nc.dram_tensor("out", [T, D], F32, kind="ExternalOutput").ap()

    b = B(nc)
    NT = T // 128
    with contextlib.ExitStack() as st:
        e = st.enter_context
        x1n = e(nc.sbuf_tensor("x1n", [128, 8 * T], BF16))
        wp = [e(nc.sbuf_tensor(f"wp{i}", [128, 2048], BF16)) for i in range(8)]
        eye = e(nc.sbuf_tensor("sb_eye", [128, 128], F32))
        ones = e(nc.sbuf_tensor("sb_ones", [128, 128], F32))
        sel8 = e(nc.sbuf_tensor("sb_sel8", [E0, E0 * 128], F32))
        projP = e(nc.sbuf_tensor("sb_projP", [128, 48], BF16))
        anchT = e(nc.sbuf_tensor("sb_anchT", [6, 6], F32))
        cscrep = e(nc.sbuf_tensor("cscrep", [128, 1], F32))
        qtok = e(nc.sbuf_tensor("qtok", [128, 8], F32))
        dott = e(nc.sbuf_tensor("dott", [128, 8], F32))
        o1 = e(nc.sbuf_tensor("o1", [128, 8], F32))
        o2 = e(nc.sbuf_tensor("o2", [128, 8], F32))
        scr = e(nc.sbuf_tensor("scr", [128, 8], F32))
        stat = e(nc.sbuf_tensor("stat", [128, 8], F32))
        onecol = e(nc.sbuf_tensor("onecol", [128, 8], F32))
        ewm = e(nc.sbuf_tensor("ewm", [128, 8], F32))
        ewT = e(nc.sbuf_tensor("ewT", [8, T], F32))
        mrep = e(nc.sbuf_tensor("mrep", [128, T], BF16))
        h_e = e(nc.sbuf_tensor("h_e", [128, 4 * T], BF16))
        yacc = e(nc.sbuf_tensor("yacc", [128, 8 * T], F32))
        u_f = e(nc.sbuf_tensor("u_f", [128, 512], F32))
        sq_f = e(nc.sbuf_tensor("sq_f", [128, 512], F32))
        th_f = e(nc.sbuf_tensor("th_f", [128, 512], F32))
        gtile = e(nc.sbuf_tensor("gtile", [128, 8 * 512], BF16))
        s1c = e(nc.sbuf_tensor("s1c", [128, 8 * 512], BF16))
        x1c = e(nc.sbuf_tensor("x1c", [128, 8 * 512], F32))
        outtok = e(nc.sbuf_tensor("outtok", [128, D], F32))
        ps8 = [e(nc.psum_tensor(f"ps8_{i}", [128, 512], F32)) for i in range(8)]

        b.dma("sync", eye[:], eye_in, writes=("eye",))
        b.dma("sync", ones[:], ones_in, writes=("ones",))
        b.dma("sync", sel8[:], sel8_in, writes=("sel8",))
        b.dma("gpsimd", projP[:], projP_in, writes=("projP",))
        b.dma("sync", anchT[:], anchT_in, writes=("anchT",))
        b.dma("sync", cscrep[0:1, 0:1], temp_in, writes=("csc0",))
        b.op("vector", lambda E: E.memset(onecol[:], 1.0), writes=("onecol",))
        b.op("vector", lambda E: E.tensor_scalar(
            cscrep[0:1, 0:1], cscrep[0:1, 0:1], 0.1, None, ALU.max),
            reads=("csc0",), writes=("csc0",))
        b.op("vector", lambda E: E.reciprocal(cscrep[0:1, 0:1], cscrep[0:1, 0:1]),
             reads=("csc0",), writes=("csc0",))
        b.op("vector", lambda E: E.tensor_scalar(
            cscrep[0:1, 0:1], cscrep[0:1, 0:1], 0.5, None, ALU.mult),
            reads=("csc0",), writes=("csc0",))
        b.op("tensor", lambda E: E.matmul(ps8[0][:, 0:1], ones[0:1, :],
                                          cscrep[0:1, 0:1], start=True, stop=True),
             reads=("ones", "csc0"), writes=("ps0",))
        b.op("scalar", lambda E: E.activation(cscrep[:], ps8[0][:, 0:1], AF.Copy),
             reads=("ps0",), writes=("cscrep",))
        b.dma("sync", x1n[:].rearrange("p (g t) -> p g t", g=8),
              x1n_hbm[:, :, :], writes=("x1n",))

        for i in range(NT):
            for k in range(8):
                b.op("tensor", lambda E, i=i, k=k: E.matmul(
                    ps8[0][:, 0:6], x1n[:, k * T + i * 128:k * T + (i + 1) * 128],
                    projP[:, k * 6:(k + 1) * 6],
                    start=(k == 0), stop=(k == 7)),
                    reads=("x1n", "projP"),
                    writes=("ps0",) if k in (0, 7) else (),
                    inc=1 if k in (0, 7) else 0)
            b.op("scalar", lambda E: E.activation(
                qtok[:, 0:6], ps8[0][:, 0:6], AF.Tanh, scale=float(1.0 / 0.3)),
                reads=("ps0",), writes=("qtok",))
            b.op("tensor", lambda E: E.matmul(ps8[1][0:6, 0:128], qtok[:, 0:6],
                                              eye[:], start=True, stop=True),
                 reads=("qtok", "eye"), writes=("ps1",))
            b.op("scalar", lambda E: E.activation(
                u_f[0:6, 0:128], ps8[1][0:6, 0:128], AF.Copy),
                reads=("ps1",), writes=("u_f",))
            b.op("tensor", lambda E: E.matmul(ps8[2][:, 0:6], u_f[0:6, 0:128],
                                              anchT[:], start=True, stop=True),
                 reads=("u_f", "anchT"), writes=("ps2",))
            b.op("vector", lambda E: E.tensor_copy(dott[:, 0:6], ps8[2][:, 0:6]),
                 reads=("ps2",), writes=("dott",))

            def top1(dst, dstk, srck, src, gcol):
                b.op("vector", lambda E: E.reduce_max(
                    stat[:, gcol:gcol + 1], src[:, 0:6], axis=AX.X),
                    reads=(srck,), writes=("stat",))
                b.op("vector", lambda E: E.tensor_scalar(
                    dst[:, 0:6], src[:, 0:6], stat[:, gcol:gcol + 1], None,
                    ALU.is_equal),
                    reads=(srck, "stat"), writes=(dstk,))
                b.op("vector", lambda E: E.tensor_tensor_scan(
                    scr[:, 0:6], onecol[:, 0:6], dst[:, 0:6], 0.0,
                    ALU.mult, ALU.add),
                    reads=(dstk, "onecol"), writes=("scr",))
                b.op("vector", lambda E: E.tensor_scalar(
                    scr[:, 0:6], scr[:, 0:6], 1.0, None, ALU.is_equal),
                    reads=("scr",), writes=("scr",))
                b.op("vector", lambda E: E.tensor_mul(
                    dst[:, 0:6], dst[:, 0:6], scr[:, 0:6]),
                    reads=(dstk, "scr"), writes=(dstk,))

            top1(o1, "o1", "dott", dott, 0)
            b.op("vector", lambda E: E.scalar_tensor_tensor(
                qtok[:, 0:6], o1[:, 0:6], -1e30, dott[:, 0:6], ALU.mult, ALU.add),
                reads=("o1", "dott"), writes=("qtok",))
            top1(o2, "o2", "qtok", qtok, 1)
            b.op("vector", lambda E: E.tensor_sub(
                stat[:, 2:3], stat[:, 0:1], stat[:, 1:2]),
                reads=("stat",), writes=("stat",))
            b.op("vector", lambda E: E.tensor_mul(
                stat[:, 2:3], stat[:, 2:3], cscrep[:, 0:1]),
                reads=("stat", "cscrep"), writes=("stat",))
            b.op("scalar", lambda E: E.activation(
                stat[:, 3:4], stat[:, 2:3], AF.Sigmoid),
                reads=("stat",), writes=("stat",))
            b.op("vector", lambda E: E.tensor_scalar(
                stat[:, 4:5], stat[:, 3:4], -1.0, 1.0, ALU.mult, ALU.add),
                reads=("stat",), writes=("stat",))
            b.op("vector", lambda E: E.tensor_scalar(
                ewm[:, 0:6], o1[:, 0:6], stat[:, 3:4], None, ALU.mult),
                reads=("o1", "stat"), writes=("ewm",))
            b.op("vector", lambda E: E.tensor_scalar(
                scr[:, 0:6], o2[:, 0:6], stat[:, 4:5], None, ALU.mult),
                reads=("o2", "stat"), writes=("scr",))
            b.op("vector", lambda E: E.tensor_add(
                ewm[:, 0:6], ewm[:, 0:6], scr[:, 0:6]),
                reads=("ewm", "scr"), writes=("ewm",))
            b.op("vector", lambda E: E.memset(ewm[:, 6:8], 0.0), writes=("ewm",))
            b.op("tensor", lambda E: E.matmul(ps8[3][0:8, 0:128], ewm[:],
                                              eye[:], start=True, stop=True),
                 reads=("ewm", "eye"), writes=("ps3",))
            b.op("scalar", lambda E, i=i: E.activation(
                ewT[:, i * 128:(i + 1) * 128], ps8[3][0:8, 0:128], AF.Copy),
                reads=("ps3",), writes=("ewT",))

        b.op("vector", lambda E: E.memset(yacc[:], 0.0), writes=("yacc",))
        for ex in range(E1):
            for nt in range(4):
                b.op("tensor", lambda E, ex=ex, nt=nt: E.matmul(
                    ps8[4][:], sel8[0:6, ex * 128:(ex + 1) * 128],
                    ewT[0:6, nt * 512:(nt + 1) * 512], start=True, stop=True),
                    reads=("sel8", "ewT"), writes=("ps4",))
                b.op("scalar", lambda E, nt=nt: E.activation(
                    mrep[:, nt * 512:(nt + 1) * 512], ps8[4][:], AF.Copy),
                    reads=("ps4",), writes=("mrep",))
            for k in range(8):
                b.dma("gpsimd", wp[k // 4][:, (k % 4) * 512:(k % 4 + 1) * 512],
                      w1f_in[ex, k * 128:(k + 1) * 128, :],
                      writes=(f"wp{k // 4}",))
            for ogl in range(4):
                for nt in range(4):
                    for k in range(8):
                        b.op("tensor", lambda E, ogl=ogl, nt=nt, k=k: E.matmul(
                            ps8[0][:],
                            wp[k // 4][:, (k % 4) * 512 + ogl * 128:
                                       (k % 4) * 512 + (ogl + 1) * 128],
                            x1n[:, k * T + nt * 512:k * T + (nt + 1) * 512],
                            start=(k == 0), stop=(k == 7)),
                            reads=(f"wp{k // 4}", "x1n"),
                            writes=("ps0",) if k in (0, 7) else (),
                            inc=1 if k in (0, 7) else 0)
                    b.op("scalar", lambda E: E.activation(u_f[:], ps8[0][:], AF.Copy),
                         reads=("ps0",), writes=("u_f",))
                    b.op("scalar", lambda E: E.activation(sq_f[:], u_f[:], AF.Square),
                         reads=("u_f",), writes=("sq_f",))
                    b.op("vector", lambda E: E.tensor_mul(sq_f[:], sq_f[:], u_f[:]),
                         reads=("sq_f", "u_f"), writes=("sq_f",))
                    b.op("vector", lambda E: E.scalar_tensor_tensor(
                        sq_f[:], sq_f[:], 0.044715, u_f[:], ALU.mult, ALU.add),
                        reads=("sq_f", "u_f"), writes=("sq_f",))
                    b.op("scalar", lambda E: E.activation(
                        th_f[:], sq_f[:], AF.Tanh, scale=0.7978845608028654),
                        reads=("sq_f",), writes=("th_f",))
                    b.op("vector", lambda E: E.tensor_scalar(
                        th_f[:], th_f[:], 1.0, 0.5, ALU.add, ALU.mult),
                        reads=("th_f",), writes=("th_f",))
                    b.op("vector", lambda E: E.tensor_mul(th_f[:], th_f[:], u_f[:]),
                         reads=("th_f", "u_f"), writes=("th_f",))
                    b.op("vector", lambda E, ogl=ogl, nt=nt: E.tensor_mul(
                        h_e[:, ogl * T + nt * 512:ogl * T + (nt + 1) * 512],
                        th_f[:], mrep[:, nt * 512:(nt + 1) * 512]),
                        reads=("th_f", "mrep"), writes=("h_e",))
            for k in range(4):
                b.dma("gpsimd", wp[4 + k // 2][:, (k % 2) * 1024:(k % 2 + 1) * 1024],
                      w2f_in[ex, k * 128:(k + 1) * 128, :],
                      writes=(f"wp{4 + k // 2}",))
            for nt in range(4):
                for k in range(4):
                    for og in range(8):
                        b.op("tensor", lambda E, k=k, og=og, nt=nt: E.matmul(
                            ps8[og][:],
                            wp[4 + k // 2][:, (k % 2) * 1024 + og * 128:
                                           (k % 2) * 1024 + (og + 1) * 128],
                            h_e[:, k * T + nt * 512:k * T + (nt + 1) * 512],
                            start=(k == 0), stop=(k == 3)),
                            reads=(f"wp{4 + k // 2}", "h_e"),
                            writes=(f"ps{og}",) if k in (0, 3) else (),
                            inc=1 if k in (0, 3) else 0)
                for og in range(8):
                    b.op("vector", lambda E, og=og, nt=nt: E.scalar_tensor_tensor(
                        yacc[:, og * T + nt * 512:og * T + (nt + 1) * 512],
                        ps8[og][:], 1.0,
                        yacc[:, og * T + nt * 512:og * T + (nt + 1) * 512],
                        ALU.mult, ALU.add),
                        reads=(f"ps{og}", "yacc"), writes=("yacc",))
        for nt in range(4):
            c0 = nt * 512
            for og in range(8):
                b.op("scalar", lambda E, og=og, nt=nt: E.activation(
                    gtile[:, og * 512:(og + 1) * 512],
                    yacc[:, og * T + nt * 512:og * T + (nt + 1) * 512], AF.Silu),
                    reads=("yacc",), writes=("gtile",))
            b.dma("sync", s1c[:].rearrange("p (g t) -> p g t", g=8),
                  s1_hbm[:, :, c0:c0 + 512], writes=("s1c",))
            b.dma("sync", x1c[:].rearrange("p (g t) -> p g t", g=8),
                  x1_hbm[:, :, c0:c0 + 512], writes=("x1c",))
            for og in range(8):
                b.op("vector", lambda E, og=og: E.tensor_mul(
                    gtile[:, og * 512:(og + 1) * 512],
                    gtile[:, og * 512:(og + 1) * 512],
                    s1c[:, og * 512:(og + 1) * 512]),
                    reads=("gtile", "s1c"), writes=("gtile",))
                b.op("vector", lambda E, og=og: E.tensor_add(
                    x1c[:, og * 512:(og + 1) * 512],
                    x1c[:, og * 512:(og + 1) * 512],
                    gtile[:, og * 512:(og + 1) * 512]),
                    reads=("gtile", "x1c"), writes=("x1c",))
            for tt in range(4):
                for jj in (0, 4):
                    for j in range(jj, jj + 4):
                        b.op("tensor", lambda E, j=j, jj=jj, tt=tt: E.transpose(
                            ps8[1][:, (j - jj) * 128:(j - jj) * 128 + 128],
                            x1c[:, j * 512 + tt * 128:j * 512 + (tt + 1) * 128],
                            eye[:]),
                            reads=("x1c", "eye"), writes=("ps1",))
                    for j in range(jj, jj + 4):
                        b.op("scalar", lambda E, j=j, jj=jj: E.activation(
                            outtok[:, j * 128:(j + 1) * 128],
                            ps8[1][:, (j - jj) * 128:(j - jj) * 128 + 128], AF.Copy),
                            reads=("ps1",), writes=("outtok",))
                b.dma("sync", out_hbm[c0 + tt * 128:c0 + (tt + 1) * 128, :],
                      outtok[:], reads=("outtok",))
        b.finish_into(st)
    return nc


def pack_rows(w):
    G = w.shape[0] // 128
    return np.ascontiguousarray(
        w.reshape(G, 128, w.shape[1]).transpose(1, 0, 2).reshape(128, -1))


def host_prep(inputs):
    f32c = lambda a: np.ascontiguousarray(a.astype(np.float32))
    c = {}
    c["eye"] = np.eye(128, dtype=np.float32)
    c["triu"] = np.triu(np.ones((128, 128), np.float32), 1)
    c["ones"] = np.ones((128, 128), np.float32)
    c["w_router_p"] = pack_rows(f32c(inputs["L0_ffn_w_router"]))
    c["wipT"] = f32c(inputs["L0_attn_in_proj_w"].T)
    c["wxpP"] = pack_rows(f32c(inputs["L0_attn_x_proj_w"].T))
    c["wdtT"] = f32c(inputs["L0_attn_dt_proj_w"].T)
    c["wopT"] = f32c(inputs["L0_attn_out_proj_w"].T)
    c["wgluT"] = f32c(inputs["L0_attn_glu_w"].T)
    c["convP"] = pack_rows(f32c(inputs["L0_attn_conv_w"][:, 0, :]))
    c["convbP"] = pack_rows(f32c(inputs["L0_attn_conv_b"][:, None]))
    c["dtbP"] = pack_rows(f32c(inputs["L0_attn_dt_proj_b"][:, None]))
    c["alogP"] = pack_rows(f32c(inputs["L0_attn_A_log"]))
    c["DP"] = pack_rows(f32c(inputs["L0_attn_D"][:, None]))
    c["glubP"] = pack_rows(f32c(inputs["L0_attn_glu_b"][:, None]))
    c["lngP"] = pack_rows(f32c(inputs["L0_attn_ln_g"][:, None]))
    c["lnbP"] = pack_rows(f32c(inputs["L0_attn_ln_b"][:, None]))
    sel16 = np.zeros((16, 16 * 128), np.float32)
    for n in range(16):
        sel16[n, n * 128:(n + 1) * 128] = 1.0
    c["sel16"] = sel16
    c["w1c"] = f32c(inputs["L0_ffn_w1"])
    c["w2c"] = f32c(inputs["L0_ffn_w2"])
    c["b1P"] = np.ascontiguousarray(
        inputs["L0_ffn_b1"].astype(np.float32).reshape(E0, 32, 128).transpose(0, 2, 1))
    c["b2c"] = f32c(inputs["L0_ffn_b2"])
    sel8 = np.zeros((E0, E0 * 128), np.float32)
    for n in range(E0):
        sel8[n, n * 128:(n + 1) * 128] = 1.0
    c["sel8"] = sel8
    c["qk_w"] = f32c(inputs["L1_attn_qk_w"])
    c["qk_b"] = f32c(inputs["L1_attn_qk_b"])
    c["lepe_w"] = f32c(inputs["L1_attn_lepe_w"])
    c["lepebP"] = pack_rows(f32c(inputs["L1_attn_lepe_b"][:, None]))
    c["projP"] = pack_rows(f32c(inputs["L1_ffn_proj_w"].T))
    c["anchT"] = f32c(inputs["L1_ffn_anchors"].T)
    c["rtemp"] = np.asarray(inputs["L1_ffn_routing_temp"], np.float32).reshape(1, 1)
    c["w1f"] = f32c(inputs["L1_ffn_w1"])
    c["w2f"] = f32c(inputs["L1_ffn_w2"])
    return c


def kernel(**inputs):
    cst = host_prep(inputs)
    x_full = np.ascontiguousarray(inputs["L0_attn_x"])

    import os
    ncA = build_A(os.environ.get("UPTO", "full"))
    in_maps = []
    for core in range(NCORE):
        in_maps.append({
            "x": np.ascontiguousarray(x_full[core]),
            "w_router": cst["w_router_p"],
            "b_router": np.ascontiguousarray(inputs["L0_ffn_b_router"]),
            "eye": cst["eye"], "triu": cst["triu"], "ones": cst["ones"],
        })
    resA = run_bass_kernel_spmd(ncA, in_maps, core_ids=list(range(NCORE)))

    ncB = build_B()
    in_maps = []
    for core in range(NCORE):
        m = {"xnT_b": resA.results[core]["xnT_b"]}
        for k in ["wipT", "wxpP", "wdtT", "wopT", "wgluT", "convP", "convbP",
                  "dtbP", "alogP", "DP", "glubP", "lngP", "lnbP", "sel16"]:
            m[k] = cst[k]
        in_maps.append(m)
    resB = run_bass_kernel_spmd(ncB, in_maps, core_ids=list(range(NCORE)))

    all_counts = np.ascontiguousarray(
        np.concatenate([resA.results[c]["counts"] for c in range(NCORE)], 0))
    ncC = build_C()
    in_maps = []
    for core in range(NCORE):
        pm = np.zeros((E0, 1), np.float32)
        pm[:core, 0] = 1.0
        in_maps.append({
            "xnT_b": resA.results[core]["xnT_b"],
            "route": resA.results[core]["route"],
            "all_counts": all_counts, "prevmask": pm,
            "w1": cst["w1c"], "b1P": cst["b1P"], "w2": cst["w2c"],
            "b2": cst["b2c"], "sel8": cst["sel8"],
            "eye": cst["eye"], "ones": cst["ones"],
        })
    resC = run_bass_kernel_spmd(ncC, in_maps, core_ids=list(range(NCORE)))

    ncD = build_D()
    in_maps = []
    for core in range(NCORE):
        in_maps.append({
            "x": np.ascontiguousarray(x_full[core]),
            "sT": resB.results[core]["sT"],
            "gT": resC.results[core]["gT"],
            "eye": cst["eye"], "ones": cst["ones"],
        })
    resD = run_bass_kernel_spmd(ncD, in_maps, core_ids=list(range(NCORE)))

    ncE = build_E()
    in_maps = []
    for core in range(NCORE):
        in_maps.append({
            "x1nT_b": resD.results[core]["x1nT_b"],
            "qk_w": cst["qk_w"], "qk_b": cst["qk_b"],
            "lepe_w": cst["lepe_w"], "lepebP": cst["lepebP"],
            "eye": cst["eye"], "ones": cst["ones"], "sel8": cst["sel8"],
        })
    resE = run_bass_kernel_spmd(ncE, in_maps, core_ids=list(range(NCORE)))
    ncF = build_F()
    in_maps = []
    for core in range(NCORE):
        in_maps.append({
            "x1nT_b": resD.results[core]["x1nT_b"],
            "x1T": resD.results[core]["x1T"],
            "s1T": resE.results[core]["s1T"],
            "projP": cst["projP"], "anchT": cst["anchT"], "rtemp": cst["rtemp"],
            "w1f": cst["w1f"], "w2f": cst["w2f"],
            "eye": cst["eye"], "ones": cst["ones"], "sel8": cst["sel8"],
        })
    resF = run_bass_kernel_spmd(ncF, in_maps, core_ids=list(range(NCORE)))

    out = np.stack([np.asarray(resF.results[c]["out"]) for c in range(NCORE)], 0)
    return out.astype(np.float32)


